# version 21
# baseline (speedup 1.0000x reference)
import contextlib
import numpy as np
import ml_dtypes
import concourse.bass as bass
import concourse.mybir as mybir
from concourse.bass_utils import run_bass_kernel_spmd

F32 = mybir.dt.float32
BF16 = mybir.dt.bfloat16
I32 = mybir.dt.int32
AF = mybir.ActivationFunctionType
ALU = mybir.AluOpType
AX = mybir.AxisListType.X

S, D, HID, G = 4096, 1024, 3072, 512
DB = D // 128
HB = HID // 128
GB = G // 128

ENGS = ("sync", "gpsimd", "tensor", "vector", "scalar")


class Rec:
    def __init__(self):
        self.prog = {k: [] for k in ENGS}
        self.cnt = {}

    def eng(self, name):
        return _EngProxy(self, name)

    def bump(self, sem_name, n=1):
        self.cnt[sem_name] = self.cnt.get(sem_name, 0) + n
        return self.cnt[sem_name]

    def val(self, sem_name):
        return self.cnt.get(sem_name, 0)


class _EngProxy:
    def __init__(self, rec, name):
        self._rec = rec
        self._name = name

    def wait(self, sem, val):
        if val > 0:
            self._rec.prog[self._name].append(lambda e: e.wait_ge(sem, val))

    def __getattr__(self, attr):
        rec, name = self._rec, self._name

        def call(*args, _inc=None, **kwargs):
            def fn(eng, attr=attr, args=args, kwargs=kwargs, _inc=_inc):
                ins = getattr(eng, attr)(*args, **kwargs)
                if _inc is not None:
                    ins.then_inc(*_inc)
            rec.prog[name].append(fn)
        return call


def _r(t, pat, **kw):
    return t.rearrange(pat, **kw)


def make_M_matrices():
    W = 8

    def pw(s, k):
        idx = s - W + k
        if idx < 0 or s < 1:
            return 0.0
        w_cnt = min(s, W)
        j = k - (W - w_cnt)
        if w_cnt > 1:
            return 0.1 + 0.9 * j / (w_cnt - 1)
        return 0.1

    Mband = np.zeros((128, 128), np.float64)
    Mprev = np.zeros((128, 128), np.float64)
    Mfirst = np.zeros((128, 128), np.float64)
    for sl in range(128):
        for k in range(W):
            jl = sl - W + k
            if jl >= 0:
                Mband[jl, sl] = pw(128 + sl, k)
                Mfirst[jl, sl] = pw(sl, k)
            else:
                Mprev[128 + jl, sl] = pw(128 + sl, k)
    return (Mband.astype(np.float32), Mprev.astype(np.float32),
            Mfirst.astype(np.float32))


def make_M_matrices_f64():
    W = 8

    def pw(s, k):
        idx = s - W + k
        if idx < 0 or s < 1:
            return 0.0
        w_cnt = min(s, W)
        j = k - (W - w_cnt)
        if w_cnt > 1:
            return 0.1 + 0.9 * j / (w_cnt - 1)
        return 0.1

    Mband = np.zeros((128, 128), np.float64)
    Mprev = np.zeros((128, 128), np.float64)
    Mfirst = np.zeros((128, 128), np.float64)
    for sl in range(128):
        for k in range(W):
            jl = sl - W + k
            if jl >= 0:
                Mband[jl, sl] = pw(128 + sl, k)
                Mfirst[jl, sl] = pw(sl, k)
            else:
                Mprev[128 + jl, sl] = pw(128 + sl, k)
    return Mband, Mprev, Mfirst


def build_fused(T=512):
    NT = T // 128
    NCH = S // T
    HQ = 12
    nc = bass.Bass("TRN2", target_bir_lowering=False, debug=False)
    x_in = nc.dram_tensor("x", [S, D], F32, kind="ExternalInput").ap()
    wkT = nc.dram_tensor("wkT", [D, D], F32, kind="ExternalInput").ap()
    wvT = nc.dram_tensor("wvT", [D, D], F32, kind="ExternalInput").ap()
    wrT = nc.dram_tensor("wrT", [D, D], F32, kind="ExternalInput").ap()
    woT = nc.dram_tensor("woT", [D, D], F32, kind="ExternalInput").ap()
    wgT = nc.dram_tensor("wgT", [D, HID], I32, kind="ExternalInput").ap()
    wuT = nc.dram_tensor("wuT", [D, HID], I32, kind="ExternalInput").ap()
    wdT = nc.dram_tensor("wdT", [HID, D], I32, kind="ExternalInput").ap()
    td_c = nc.dram_tensor("td_c", [128, DB], F32, kind="ExternalInput").ap()
    tf_c = nc.dram_tensor("tf_c", [128, DB], F32, kind="ExternalInput").ap()
    guid_d = nc.dram_tensor("guid", [G, D], F32, kind="ExternalInput").ap()
    wlT_d = nc.dram_tensor("wlT", [D, D], F32, kind="ExternalInput").ap()
    ident_d = nc.dram_tensor("ident", [128, 128], BF16, kind="ExternalInput").ap()
    cvec_d = nc.dram_tensor("cvec_in", [128, 4], F32, kind="ExternalInput").ap()
    mband_d = nc.dram_tensor("mband", [128, 128], F32, kind="ExternalInput").ap()
    mprev_d = nc.dram_tensor("mprev", [128, 128], F32, kind="ExternalInput").ap()
    mfirst_d = nc.dram_tensor("mfirst", [128, 128], F32, kind="ExternalInput").ap()
    mlo_d = nc.dram_tensor("mlo", [128, 384], F32, kind="ExternalInput").ap()
    onesc_d = nc.dram_tensor("onesc", [128, 1], F32, kind="ExternalInput").ap()
    onesr_d = nc.dram_tensor("onesr", [1, 128], F32, kind="ExternalInput").ap()
    out_d = nc.dram_tensor("out", [S, D], F32, kind="ExternalOutput").ap()
    wk_i = nc.dram_tensor("wk_i", [128, DB, D], BF16, kind="Internal").ap()
    wv_i = nc.dram_tensor("wv_i", [128, DB, D], BF16, kind="Internal").ap()
    wr_i = nc.dram_tensor("wr_i", [128, DB, D], BF16, kind="Internal").ap()
    wo_i = nc.dram_tensor("wo_i", [128, DB, D], BF16, kind="Internal").ap()
    wg_i = nc.dram_tensor("wg_i", [128, DB, HID], BF16, kind="Internal").ap()
    wu_i = nc.dram_tensor("wu_i", [128, DB, HID], BF16, kind="Internal").ap()
    wd_i = nc.dram_tensor("wd_i", [128, HB, D], BF16, kind="Internal").ap()

    es = contextlib.ExitStack()
    sbt = lambda name, shape, dt: es.enter_context(nc.sbuf_tensor(name, shape, dt))
    ring = [sbt("ring0", [128, 12288], BF16), sbt("ring1", [128, 12288], BF16)]
    argA1 = sbt("argA1", [128, 4096], F32)
    argA2 = sbt("argA2", [128, 12288], BF16)
    argB = sbt("argB", [128, 6144], BF16)
    argC = sbt("argC", [128, 4096], BF16)
    xa = sbt("xa", [128, NT, D], F32)
    xb = sbt("xb", [128, NT, D], F32)
    xn0T = sbt("xn0T", [128, DB, T], BF16)
    trs = sbt("trs", [128, DB, T], BF16)
    zcarry = sbt("zcarry", [128, DB, 1], F32)
    zs = _r(argA1[:, 0:DB * T], "p (i t) -> p i t", i=DB)
    stg2 = _r(argA1[:, 0:4 * D], "p (g e) -> p g e", g=4)
    tr1 = argA1[:, 0:D]
    tg = argA1[:, D:2 * D]
    x2t = tg
    mstg = argA1[:, 2 * D:2 * D + 384]
    mlo_s = argA1[:, 2 * D + 384:2 * D + 768]
    h_sb = _r(argA2[:, 0:HB * T], "p (i t) -> p i t", i=HB)
    gnrm = argA2[:, 0:D]
    de = _r(argA2[:, 0:DB * T], "p (i t) -> p i t", i=DB)
    xn1T = _r(argA2[:, DB * T:2 * DB * T], "p (i t) -> p i t", i=DB)
    difs = [argB[:, 5120:5120 + D], argA2[:, 2 * DB * T:2 * DB * T + D],
            argB[:, 4096:4096 + D], argB[:, 0:D]]
    g1t = argA2[:, 2 * DB * T + D:2 * DB * T + 2 * D]
    gw = argA2[:, 2 * DB * T + 2 * D:2 * DB * T + 2 * D + G]
    gwTv = argA2[:, 2 * DB * T + 2 * D + G:2 * DB * T + 2 * D + 2 * G]
    exp_sb = argA2[:, 2 * DB * T + 2 * D + 2 * G:2 * DB * T + 2 * D + 3 * G]
    kbf = _r(argB[:, 0:DB * T], "p (i t) -> p i t", i=DB)
    s_bf = _r(argB[:, 0:NT * D], "p (j e) -> p j e", j=NT)
    up_sb = _r(argB[:, 0:HQ * T], "p (i t) -> p i t", i=HQ)
    sq4 = _r(argB[:, 0:NT * D], "p (j e) -> p j e", j=NT)
    vbf = _r(argC[:, 0:DB * T], "p (i t) -> p i t", i=DB)
    t2 = argC[:, 2560:2560 + D]
    sgt = argC[:, 0:T]
    g0ts = [argC[:, T:T + D], argC[:, 2560:2560 + D]]
    t1ts = [argC[:, T + D:T + 2 * D], argB[:, 4096:4096 + D]]
    xnt = sbt("xnt", [128, D], BF16)
    stats = sbt("stats", [128, 64], F32)
    statL = sbt("statL", [128, 40], F32)
    mwb4 = sbt("mwb4", [128, 8], BF16)
    tdc = sbt("tdc", [128, DB], F32)
    tfc = sbt("tfc", [128, DB], F32)
    dec = sbt("dec", [128, DB], F32)
    eu = sbt("eu", [128, DB], F32)
    deprev = sbt("deprev", [128, DB, 128], BF16)
    xn1Tprev = sbt("xn1Tprev", [128, DB, 128], BF16)
    xn1last = sbt("xn1last", [128, DB, 1], BF16)
    x1prev = sbt("x1prev", [128, D], F32)
    gwT = _r(gwTv, "p (i t) -> p i t", i=GB)
    bw = [sbt("bw0", [128, D], BF16), sbt("bw1", [128, D], BF16)]
    mwb = sbt("mwb", [128, 2], BF16)
    guid_nT = sbt("guid_nT", [128, DB, G], BF16)
    guidb = sbt("guidb", [128, GB, D], BF16)
    wl_tern = sbt("wl_tern", [128, DB, D], BF16)
    gn2 = sbt("gn2", [128, GB], F32)
    invg = sbt("invg", [128, GB], F32)
    aw = sbt("aw", [128, DB], F32)
    scl = sbt("scl", [1, 4], F32)
    scaleB = sbt("scaleB", [128, 1], F32)
    rscB = sbt("rscB", [128, 1], F32)
    identb = sbt("identb", [128, 128], BF16)
    cvec = sbt("cvec", [128, 4], F32)
    mbandb = sbt("mbandb", [128, 128], BF16)
    mprevb = sbt("mprevb", [128, 128], BF16)
    mfirstb = sbt("mfirstb", [128, 128], BF16)

    onescb = sbt("onescb", [128, 1], F32)
    onesrb = sbt("onesrb", [1, 128], F32)

    psm = lambda name, shape, dt: es.enter_context(nc.psum_tensor(name, shape, dt))
    pa = psm("pa", [128, 1024], F32)
    pb = psm("pb", [128, 1024], F32)
    pc = psm("pc", [128, 1024], F32)
    pd = psm("pd", [128, 512], F32)
    pt = psm("pt", [128, 1024], BF16)

    sem = {}
    for s_ in ("mi", "x0", "x1s", "w0", "w1", "g0", "g1", "gc", "gl", "c0", "c1", "t", "v", "a", "st"):
        sem[s_] = es.enter_context(nc.semaphore(s_))
    sem_mi, sem_t, sem_v, sem_a, sem_st, sem_gc = (
        sem["mi"], sem["t"], sem["v"], sem["a"], sem["st"], sem["gc"])
    sem_x = [sem["x0"], sem["x1s"]]
    sem_gl = sem["gl"]
    sem_c = [sem["c0"], sem["c1"]]
    sem_w = [sem["w0"], sem["w1"]]
    sem_g = [sem["g0"], sem["g1"]]

    R = Rec()
    E = {k: R.eng(k) for k in ENGS}

    psum_guard = {}

    def psum_use(key):
        g = psum_guard.get(key)
        if g is not None:
            E["tensor"].wait(g[0], g[1])

    def psum_set(key, sm, count):
        psum_guard[key] = (sm, count)

    buf_guard = {}

    def buf_use(eng, name):
        g = buf_guard.get(name)
        if g is not None:
            E[eng].wait(g[0], g[1])

    def buf_set(name, sm, count):
        buf_guard[name] = (sm, count)

    def vfence():
        E["vector"].wait(sem_v, R.val("v"))

    def afence():
        E["scalar"].wait(sem_a, R.val("a"))

    for (dst, srcp) in [(identb[:], ident_d), (cvec[:], cvec_d),
                        (mstg[:, 0:128], mband_d), (mstg[:, 128:256], mprev_d),
                        (mstg[:, 256:384], mfirst_d), (onescb[:], onesc_d),
                        (onesrb[:], onesr_d), (tdc[:], td_c), (tfc[:], tf_c),
                        (mlo_s, mlo_d)]:
        E["sync"].dma_start(out=dst, in_=srcp, _inc=(sem_mi, 16)); R.bump("mi", 16)
    mi_done0 = R.val("mi")
    E["vector"].wait(sem_mi, mi_done0)
    E["vector"].tensor_add(mstg, mstg, mlo_s, _inc=(sem_v, 1)); R.bump("v")
    E["vector"].wait(sem_v, R.val("v"))
    E["vector"].tensor_copy(mbandb[:], mstg[:, 0:128])
    E["vector"].tensor_copy(mprevb[:], mstg[:, 128:256])
    E["vector"].tensor_copy(mfirstb[:], mstg[:, 256:384], _inc=(sem_v, 1))
    v_mdone = R.bump("v")
    mi_done = R.val("mi")
    E["sync"].dma_start(out=xa[:], in_=_r(x_in[0:T, :], "(j p) e -> p j e", p=128),
                        _inc=(sem_x[0], 16))
    x_ld = {0: (0, R.bump("x0", 16))}

    E["vector"].wait(sem_mi, mi_done)
    E["vector"].memset(zcarry[:], 0.0, _inc=(sem_v, 1)); R.bump("v")
    E["scalar"].wait(sem_mi, mi_done)
    E["scalar"].activation(dec[:], tdc[:], AF.Exp, _inc=(sem_a, 1)); R.bump("a")
    afence()
    E["scalar"].activation(dec[:], dec[:], AF.Exp, scale=-1.0)
    E["scalar"].activation(eu[:], tfc[:], AF.Exp, _inc=(sem_a, 1)); R.bump("a")

    cvt_src = {}
    for (srcp, dst) in [(wkT, wk_i), (wvT, wv_i), (wrT, wr_i), (woT, wo_i),
                        (wgT, wg_i), (wuT, wu_i)]:
        cvt_src[dst.tensor.name] = _r(srcp, "(i p) e -> p i e", p=128)
    cvt_src[wd_i.tensor.name] = _r(wdT, "(i p) e -> p i e", p=128)
    slot = 0
    piece_store = {}

    free_at = {}
    state = {"slot": slot}
    SEMMAP = {"st": sem_st, "t": sem_t, "g0": sem_g[0], "g1": sem_g[1],
              "w0": sem_w[0], "w1": sem_w[1], "c0": sem_c[0], "c1": sem_c[1]}

    def _wait_free(eng, sl):
        for (s_, v_) in free_at.get(sl - 2, ()):
            E[eng].wait(SEMMAP[s_], v_)

    def ring_load(src_ap, nb, cols, cvt=None):
        sl = state["slot"]; state["slot"] += 1
        par = sl % 2
        v = ring[par][:, 0:nb * cols].rearrange("p (i e) -> p i e", i=nb)
        if cvt is None:
            _wait_free("sync", sl)
            E["sync"].dma_start(out=v, in_=src_ap, _inc=(sem_w[par], 16))
            ldt = ("w%d" % par, R.bump("w%d" % par, 16))
            free_at[sl] = []
        else:
            _wait_free("gpsimd", sl)
            E["gpsimd"].dma_start(out=v, in_=src_ap, _inc=(sem_g[par], 16))
            ldt = ("g%d" % par, R.bump("g%d" % par, 16))
            E["sync"].wait(sem_g[par], ldt[1])
            E["sync"].dma_start(out=cvt, in_=v, _inc=(sem_c[par], 16))
            free_at[sl] = [("c%d" % par, R.bump("c%d" % par, 16))]
        return sl, ldt, v

    def ring_release(sl):
        free_at[sl] = free_at.get(sl, []) + [("t", R.val("t"))]

    def emit_l1_prep():
        E["scalar"].wait(sem_a, R.val("a"))
        E["gpsimd"].wait(sem_v, v_mdone)
        E["gpsimd"].dma_start(out=stg2[:], in_=_r(guid_d, "(i p) e -> p i e", p=128),
                              _inc=(sem_gl, 16)); R.bump("gl", 16)
        gl_guid = R.val("gl")
        E["scalar"].wait(sem_gl, gl_guid)
        E["vector"].wait(sem_gl, gl_guid)
        for gi in range(GB):
            E["scalar"].activation(sq4[:, gi % NT, :], stg2[:, gi, :], AF.Square,
                                   accum_out=gn2[:, gi:gi + 1], _inc=(sem_a, 1))
            R.bump("a")
            if gi % NT == NT - 1:
                afence()
        a_gn2 = R.val("a")
        E["scalar"].wait(sem_a, a_gn2)
        E["scalar"].activation(gn2[:], gn2[:], AF.Ln, _inc=(sem_a, 1)); R.bump("a")
        afence()
        E["scalar"].activation(gn2[:], gn2[:], AF.Exp, scale=0.5, _inc=(sem_a, 1))
        a_gn = R.bump("a")
        E["vector"].wait(sem_a, a_gn)
        E["vector"].tensor_scalar_max(invg[:], gn2[:], 1e-8, _inc=(sem_v, 1)); R.bump("v")
        vfence()
        E["vector"].reciprocal(invg[:], invg[:], _inc=(sem_v, 1)); R.bump("v")
        vfence()
        for gi in range(GB):
            buf_use("vector", "gnrm")
            E["vector"].tensor_scalar(gnrm, stg2[:, gi, :], invg[:, gi:gi + 1],
                                      None, ALU.mult)
            E["vector"].tensor_copy(guidb[:, gi, :], stg2[:, gi, :], _inc=(sem_v, 1))
            v_gn = R.bump("v")
            E["tensor"].wait(sem_v, v_gn)
            psum_use("pt")
            for i in range(DB):
                kw = {} if i < DB - 1 else {"_inc": (sem_t, 1)}
                E["tensor"].transpose(pt[:, i * 128:(i + 1) * 128],
                                      gnrm[:, i * 128:(i + 1) * 128], identb[:], **kw)
            t_tp = R.bump("t")
            buf_set("gnrm", sem_t, t_tp)
            E["vector"].wait(sem_t, t_tp)
            E["vector"].tensor_copy(guid_nT[:, :, gi * 128:(gi + 1) * 128],
                                    _r(pt[:], "p (i t) -> p i t", i=DB), _inc=(sem_v, 1))
            psum_set("pt", sem_v, R.bump("v"))

        for hf in range(2):
            E["gpsimd"].wait(sem_v, R.val("v"))
            E["gpsimd"].wait(sem_a, R.val("a"))
            E["gpsimd"].dma_start(out=stg2[:],
                                in_=_r(wlT_d, "(i p) e -> p i e", p=128)[:, 4 * hf:4 * hf + 4, :],
                                _inc=(sem_gl, 16))
            mi_wl = R.bump("gl", 16)
            E["scalar"].wait(sem_gl, mi_wl)
            for q in range(4):
                i = 4 * hf + q
                E["scalar"].activation(sq4[:, q % NT, :], stg2[:, q, :], AF.Abs,
                                       accum_out=aw[:, i:i + 1], _inc=(sem_a, 1))
                R.bump("a")
                if q % NT == NT - 1:
                    afence()
        a_aw = R.val("a")
        E["vector"].wait(sem_a, a_aw)
        E["vector"].reduce_sum(gn2[:, 0:1], aw[:], axis=AX, _inc=(sem_v, 1))
        v_aw = R.bump("v")
        E["tensor"].wait(sem_v, v_aw)
        psum_use("pd")
        E["tensor"].matmul(pd[0:1, 0:1], onescb[:], gn2[:, 0:1], start=True, stop=True,
                           _inc=(sem_t, 1))
        t_tot = R.bump("t")
        E["vector"].wait(sem_t, t_tot)
        E["vector"].tensor_scalar(scl[0:1, 0:1], pd[0:1, 0:1], 1.0 / (D * D), None,
                                  ALU.mult, _inc=(sem_v, 1))
        psum_set("pd", sem_v, R.bump("v"))
        vfence()
        E["vector"].tensor_scalar_max(scl[0:1, 1:2], scl[0:1, 0:1], 1e-5, _inc=(sem_v, 1))
        R.bump("v")
        vfence()
        E["vector"].reciprocal(scl[0:1, 2:3], scl[0:1, 1:2], _inc=(sem_v, 1))
        v_scl = R.bump("v")
        E["tensor"].wait(sem_v, v_scl)
        psum_use("pd")
        E["tensor"].matmul(pd[:, 1:2], onesrb[:], scl[0:1, 1:2], start=True, stop=True)
        E["tensor"].matmul(pd[:, 2:3], onesrb[:], scl[0:1, 2:3], start=True, stop=True,
                           _inc=(sem_t, 1))
        t_bc = R.bump("t")
        E["vector"].wait(sem_t, t_bc)
        E["vector"].tensor_copy(scaleB[:], pd[:, 1:2])
        E["vector"].tensor_copy(rscB[:], pd[:, 2:3], _inc=(sem_v, 1))
        v_bc = R.bump("v")
        psum_set("pd", sem_v, v_bc)
        E["vector"].wait(sem_v, v_bc)
        for hf in range(2):
            E["gpsimd"].wait(sem_v, R.val("v"))
            E["gpsimd"].wait(sem_a, R.val("a"))
            E["gpsimd"].dma_start(out=stg2[:],
                                in_=_r(wlT_d, "(i p) e -> p i e", p=128)[:, 4 * hf:4 * hf + 4, :],
                                _inc=(sem_gl, 16))
            mi_wl = R.bump("gl", 16)
            E["vector"].wait(sem_gl, mi_wl)
            for q in range(4):
                i = 4 * hf + q
                zscr = _r(xn1T[:, 0:2, :], "p a t -> p (a t)")
                zascr = _r(xn1T[:, 2:4, :], "p a t -> p (a t)")
                E["vector"].tensor_scalar(zscr, stg2[:, q, :], rscB[:, 0:1], None,
                                          ALU.mult, _inc=(sem_v, 1))
                v_z = R.bump("v")
                E["scalar"].wait(sem_v, v_z)
                E["scalar"].activation(difs[0], zscr, AF.Sign)
                E["scalar"].activation(zascr, zscr, AF.Abs, _inc=(sem_a, 1))
                a_as = R.bump("a")
                E["vector"].wait(sem_a, a_as)
                E["vector"].tensor_single_scalar(g1t, zascr, 0.5, ALU.is_gt,
                                                 _inc=(sem_v, 1))
                R.bump("v")
                vfence()
                E["vector"].tensor_mul(wl_tern[:, i, :], g1t, difs[0], _inc=(sem_v, 1))
                R.bump("v")
                vfence()


    def ln_sq_tile(xcur, j, off):
        cs = off + 8 * j
        buf_use("scalar", "xnt_sq")
        E["scalar"].activation(xnt[:], xcur[:, j, :], AF.Square,
                               accum_out=stats[:, cs + 1:cs + 2], _inc=(sem_a, 1))
        a_ = R.bump("a")
        buf_set("xnt_sq", sem_a, a_)
        E["vector"].reduce_sum(stats[:, cs + 0:cs + 1], xcur[:, j, :], axis=AX,
                               _inc=(sem_v, 1))
        return a_, R.bump("v")

    def ln_finish(off):
        E["vector"].wait(sem_v, R.val("v"))
        E["vector"].wait(sem_a, R.val("a"))
        for j in range(NT):
            cs = off + 8 * j
            kw = {} if j < NT - 1 else {"_inc": (sem_v, 1)}
            E["vector"].tensor_scalar(stats[:, cs + 2:cs + 3], stats[:, cs + 0:cs + 1],
                                      -1.0 / D, None, ALU.mult, **kw)
        v_mneg = R.bump("v")
        E["vector"].wait(sem_v, v_mneg)
        for j in range(NT):
            cs = off + 8 * j
            kw = {} if j < NT - 1 else {"_inc": (sem_v, 1)}
            E["vector"].scalar_tensor_tensor(stats[:, cs + 3:cs + 4],
                                             stats[:, cs + 0:cs + 1],
                                             stats[:, cs + 2:cs + 3],
                                             stats[:, cs + 1:cs + 2],
                                             ALU.mult, ALU.add, **kw)
        v_var = R.bump("v")
        E["scalar"].wait(sem_v, v_var)
        var_cols = _r(stats[:, off:off + 8 * NT], "p (j k) -> p j k", j=NT)[:, :, 3:4]
        E["scalar"].activation(var_cols, var_cols, AF.Ln, scale=1.0 / D,
                               bias=cvec[:, 0:1], _inc=(sem_a, 1))
        a_ln = R.bump("a")
        E["scalar"].wait(sem_a, a_ln)
        E["scalar"].activation(var_cols, var_cols, AF.Exp, scale=-0.5, _inc=(sem_a, 1))
        return R.bump("a")

    def ln_apply(xcur, off, xnT, a_rstd):
        E["vector"].wait(sem_a, a_rstd)
        xnT_done = None
        for j in range(NT):
            cs = off + 8 * j
            E["vector"].tensor_scalar(xnt[:], xcur[:, j, :], stats[:, cs + 2:cs + 3],
                                      stats[:, cs + 3:cs + 4], ALU.add, ALU.mult,
                                      _inc=(sem_v, 1))
            v_apply = R.bump("v")
            E["tensor"].wait(sem_v, v_apply)
            psum_use("pt")
            for i in range(DB):
                kw = {} if i < DB - 1 else {"_inc": (sem_t, 1)}
                E["tensor"].transpose(pt[:, i * 128:(i + 1) * 128],
                                      xnt[:, i * 128:(i + 1) * 128], identb[:], **kw)
            t_tp = R.bump("t")
            E["vector"].wait(sem_t, t_tp)
            E["vector"].tensor_copy(xnT[:, :, j * 128:(j + 1) * 128],
                                    _r(pt[:], "p (i t) -> p i t", i=DB),
                                    _inc=(sem_v, 1))
            xnT_done = R.bump("v")
            psum_set("pt", sem_v, xnT_done)
        return xnT_done

    pt_seq = [0]
    carry_v = {}

    def tile_pre(dcm, idx, v_need=None):
        E["tensor"].wait(sem_v, v_need if v_need is not None else R.val("v"))
        psum_use("pt")
        buf_use("tensor", "dif%d" % idx)
        for i in range(DB):
            kw = {} if i < DB - 1 else {"_inc": (sem_t, 1)}
            E["tensor"].transpose(pt[:, i * 128:(i + 1) * 128], dcm[:, i, :],
                                  identb[:], **kw)
        t_tp = R.bump("t")
        buf_use("vector", "difv%d" % idx)
        E["vector"].wait(sem_t, t_tp)
        E["vector"].tensor_copy(difs[idx], pt[:], _inc=(sem_v, 1))
        v_dif = R.bump("v")
        psum_set("pt", sem_v, v_dif)
        E["scalar"].wait(sem_v, v_dif)
        buf_use("scalar", "xnt_sq")
        E["scalar"].activation(xnt[:], difs[idx], AF.Square,
                               accum_out=statL[:, 0 + idx:1 + idx], _inc=(sem_a, 1))
        a_m2 = R.bump("a")
        buf_set("xnt_sq", sem_a, a_m2)
        return a_m2

    def stats_batch(nw, o=0, a_need=None):
        c0, c1_, c2_, c3, c4, c5, c6, c7 = (
            statL[:, o:o + nw], statL[:, 4 + o:4 + o + nw],
            statL[:, 8 + o:8 + o + nw], statL[:, 12 + o:12 + o + nw],
            statL[:, 16 + o:16 + o + nw], statL[:, 20 + o:20 + o + nw],
            statL[:, 24 + o:24 + o + nw], statL[:, 28 + o:28 + o + nw])
        E["vector"].wait(sem_a, a_need if a_need is not None else R.val("a"))
        E["vector"].tensor_scalar_max(c1_, c0, 1e-24, _inc=(sem_v, 1))
        v1 = R.bump("v")
        E["scalar"].wait(sem_v, v1)
        E["scalar"].activation(c2_, c1_, AF.Ln, _inc=(sem_a, 1))
        aln = R.bump("a")
        E["scalar"].wait(sem_a, aln)
        E["scalar"].activation(c3, c2_, AF.Exp, scale=0.5)
        E["scalar"].activation(c4, c2_, AF.Exp, scale=-0.5, _inc=(sem_a, 1))
        amag = R.bump("a")
        E["scalar"].wait(sem_a, amag)
        E["scalar"].activation(c5, c3, AF.Exp, scale=-4.0, _inc=(sem_a, 1))
        au = R.bump("a")
        E["vector"].wait(sem_a, au)
        E["vector"].tensor_single_scalar(c6, c0, 1e-12, ALU.is_gt)
        E["vector"].tensor_scalar(c7, c5, 1.0, None, ALU.add, _inc=(sem_v, 1))
        v2 = R.bump("v")
        E["vector"].wait(sem_v, v2)
        E["vector"].tensor_mul(c4, c4, c6)
        E["vector"].reciprocal(c7, c7, _inc=(sem_v, 1))
        v3 = R.bump("v")
        E["vector"].wait(sem_v, v3)
        E["vector"].tensor_scalar(c5, c5, -1.0, 1.0, ALU.mult, ALU.add)
        E["vector"].tensor_scalar(c6, c4, 2.0, None, ALU.mult, _inc=(sem_v, 1))
        v4 = R.bump("v")
        E["vector"].wait(sem_v, v4)
        E["vector"].tensor_mul(c5, c5, c7, _inc=(sem_v, 1))
        v5 = R.bump("v")
        E["vector"].wait(sem_v, v5)
        E["vector"].scalar_tensor_tensor(c2_, c5, 0.6, c4, ALU.mult, ALU.mult)
        E["vector"].tensor_scalar(c3, c5, 0.4, None, ALU.mult)
        buf_use("vector", "mwb4")
        E["vector"].tensor_copy(mwb4[:, 1 + o:1 + o + nw], c5, _inc=(sem_v, 1))
        return R.bump("v")

    def dens_batch(nw, v_stats, first_tile_mfirst, o=0):
        E["tensor"].wait(sem_v, v_stats)
        psum_use("pd")
        if first_tile_mfirst:
            E["tensor"].matmul(pd[:, 1:2], mfirstb[:], mwb4[:, 1:2],
                               start=True, stop=True)
            E["tensor"].matmul(pd[:, 2:1 + nw], mbandb[:], mwb4[:, 2:1 + nw],
                               start=True, stop=False)
            E["tensor"].matmul(pd[:, 2:1 + nw], mprevb[:], mwb4[:, 1:nw],
                               start=False, stop=True, _inc=(sem_t, 1))
        else:
            E["tensor"].matmul(pd[:, 1 + o:1 + o + nw], mbandb[:],
                               mwb4[:, 1 + o:1 + o + nw],
                               start=True, stop=False)
            E["tensor"].matmul(pd[:, 1 + o:1 + o + nw], mprevb[:],
                               mwb4[:, o:o + nw],
                               start=False, stop=True, _inc=(sem_t, 1))
        t_den = R.bump("t")
        E["vector"].wait(sem_t, t_den)
        E["vector"].tensor_scalar_max(statL[:, 36 + o:36 + o + nw],
                                      pd[:, 1 + o:1 + o + nw], 1e-8,
                                      _inc=(sem_v, 1))
        v_dmax = R.bump("v")
        psum_set("pd", sem_v, v_dmax)
        E["vector"].wait(sem_v, v_dmax)
        E["vector"].reciprocal(statL[:, 36 + o:36 + o + nw],
                               statL[:, 36 + o:36 + o + nw],
                               _inc=(sem_v, 1))
        return R.bump("v")

    def tile_dots_mm(idx, dcm):
        psum_use("pd")
        for i in range(DB):
            kw = {} if i < DB - 1 else {"_inc": (sem_t, 1)}
            E["tensor"].matmul(pd[:], dcm[:, i, :], guid_nT[:, i, :],
                               start=(i == 0), stop=(i == DB - 1), **kw)
        return R.bump("t")

    def tile_exp(idx, t_dots, v_stats, barrier_t=False):
        E["scalar"].wait(sem_t, R.val("t") if barrier_t else t_dots)
        E["scalar"].wait(sem_v, v_stats)
        buf_use("scalar", "exp_sb")
        E["scalar"].activation(exp_sb, pd[:], AF.Exp,
                               scale=statL[:, 24 + idx:25 + idx],
                               accum_out=statL[:, 28 + idx:29 + idx], _inc=(sem_a, 1))
        a_exp = R.bump("a")
        psum_set("pd", sem_a, a_exp)
        E["vector"].wait(sem_a, a_exp)
        E["vector"].reciprocal(statL[:, 32 + idx:33 + idx],
                               statL[:, 28 + idx:29 + idx], _inc=(sem_v, 1))
        v_rs = R.bump("v")
        E["vector"].wait(sem_v, v_rs)
        E["vector"].tensor_mul(statL[:, 32 + idx:33 + idx],
                               statL[:, 32 + idx:33 + idx],
                               statL[:, 12 + idx:13 + idx], _inc=(sem_v, 1))
        v_c2r = R.bump("v")
        return a_exp, v_c2r

    def tile_gw(idx, a_exp):
        E["tensor"].wait(sem_a, a_exp)
        psum_use("pt")
        for gi in range(GB):
            kw = {} if gi < GB - 1 else {"_inc": (sem_t, 1)}
            E["tensor"].transpose(pt[:, gi * 128:(gi + 1) * 128],
                                  exp_sb[:, gi * 128:(gi + 1) * 128], identb[:], **kw)
        t_gwt = R.bump("t")
        buf_set("exp_sb", sem_t, t_gwt)
        buf_use("vector", "gwT")
        E["vector"].wait(sem_t, t_gwt)
        E["vector"].tensor_copy(gwT[:], _r(pt[:, 0:512], "p (i t) -> p i t", i=GB),
                                _inc=(sem_v, 1))
        v_gwT = R.bump("v")
        psum_set("pt", sem_v, v_gwT)
        E["tensor"].wait(sem_v, v_gwT)
        psum_use("pa")
        for gi in range(GB):
            for h2 in range(2):
                kw = {} if not (gi == GB - 1 and h2 == 1) else {"_inc": (sem_t, 1)}
                E["tensor"].matmul(pa[:, h2 * 512:(h2 + 1) * 512], gwT[:, gi, :],
                                   guidb[:, gi, h2 * 512:(h2 + 1) * 512],
                                   start=(gi == 0), stop=(gi == GB - 1), **kw)
        t_inf = R.bump("t")
        buf_set("gwT", sem_t, t_inf)
        return t_inf

    def tile_t2blend(idx, sl, t_inf, v_c2r, gc_need=None):
        bwc = bw[sl]
        if gc_need is not None:
            E["vector"].wait(sem_gc, gc_need)
        E["vector"].wait(sem_t, t_inf)
        E["vector"].wait(sem_v, v_c2r)
        buf_use("vector", "t2")
        E["vector"].tensor_scalar(t2, pa[:], statL[:, 32 + idx:33 + idx], None,
                                  ALU.mult, _inc=(sem_v, 1))
        v_t2 = R.bump("v")
        psum_set("pa", sem_v, v_t2)
        E["vector"].wait(sem_v, v_t2)
        buf_use("vector", "bw%d" % sl)
        E["vector"].scalar_tensor_tensor(bwc[:], difs[idx], statL[:, 8 + idx:9 + idx],
                                         t2, ALU.mult, ALU.add, _inc=(sem_v, 1))
        v_bw = R.bump("v")
        buf_set("difv%d" % idx, sem_v, v_bw)
        buf_set("t2", sem_v, v_bw)
        return v_bw

    def tile_numf1(idx, sl, dcm, x1src, xnTsrc, mmat, use_prev, out_rows,
                   v_rden, v_bw):
        bwc, bwh = bw[sl], bw[1 - sl]
        E["tensor"].wait(sem_v, v_bw)
        psum_use("pb")
        for h2 in range(2):
            kw = {} if (use_prev or h2 == 0) else {"_inc": (sem_t, 1)}
            E["tensor"].matmul(pb[:, h2 * 512:(h2 + 1) * 512], mmat[:],
                               bwc[:, h2 * 512:(h2 + 1) * 512],
                               start=True, stop=(not use_prev), **kw)
        if use_prev:
            for h2 in range(2):
                kw = {} if h2 == 0 else {"_inc": (sem_t, 1)}
                E["tensor"].matmul(pb[:, h2 * 512:(h2 + 1) * 512], mprevb[:],
                                   bwh[:, h2 * 512:(h2 + 1) * 512],
                                   start=False, stop=True, **kw)
        t_num = R.bump("t")
        buf_set("bw%d" % (1 - sl), sem_t, t_num)
        psum_use("pc")
        for i in range(DB):
            for h2 in range(2):
                kw = {} if not (i == DB - 1 and h2 == 1) else {"_inc": (sem_t, 1)}
                E["tensor"].matmul(pc[:, h2 * 512:(h2 + 1) * 512], xnTsrc[:, i, :],
                                   wl_tern[:, i, h2 * 512:(h2 + 1) * 512],
                                   start=(i == 0), stop=(i == DB - 1), **kw)
        t_f1 = R.bump("t")
        E["scalar"].wait(sem_t, t_f1)
        buf_use("scalar", "g1t")
        E["scalar"].activation(g1t, pc[:], AF.Silu, scale=scaleB[:, 0:1],
                               _inc=(sem_a, 1))
        a_g1 = R.bump("a")
        psum_set("pc", sem_a, a_g1)
        E["vector"].wait(sem_v, v_rden)
        E["vector"].wait(sem_t, t_num)
        buf_use("vector", "tr1")
        E["vector"].tensor_scalar(tr1, pb[:], statL[:, 36 + idx:37 + idx], None,
                                  ALU.mult, _inc=(sem_v, 1))
        v_tr1 = R.bump("v")
        psum_set("pb", sem_v, v_tr1)
        E["gpsimd"].wait(sem_v, v_tr1)
        E["gpsimd"].wait(sem_a, a_g1)
        buf_use("gpsimd", "tg")
        E["gpsimd"].tensor_mul(tg, tr1, g1t, _inc=(sem_gc, 1))
        g_tg = R.bump("gc")
        buf_set("tr1", sem_gc, g_tg)
        buf_set("g1t", sem_gc, g_tg)
        E["vector"].wait(sem_gc, g_tg)
        E["vector"].tensor_add(tg, tg, x1src, _inc=(sem_v, 1))
        v_x2 = R.bump("v")
        E["sync"].wait(sem_v, v_x2)
        E["sync"].dma_start(out=_r(out_d[out_rows:out_rows + 128, :], "(o p) e -> p (o e)", p=128),
                            in_=tg, _inc=(sem_st, 16))
        st_c = R.bump("st", 16)
        buf_set("tg", sem_st, st_c)

    def tile_dots(idx, dcm, v_stats):
        return tile_exp(idx, tile_dots_mm(idx, dcm), v_stats)

    def tile_pipeline(tiles, v_stats, v_rden, idx0=0):
        n = len(tiles)
        st = {0: tile_dots(idx0, tiles[0][0], v_stats)}
        ti = {0: tile_gw(idx0, st[0][0])}
        for i in range(n):
            idx = idx0 + i
            sl = pt_seq[0] % 2
            v_bw = tile_t2blend(idx, sl, ti[i], st[i][1])
            if i + 1 < n:
                st[i + 1] = tile_dots(idx + 1, tiles[i + 1][0], v_stats)
            dcm, x1src, xnTsrc, mmat, use_prev, out_rows = tiles[i]
            tile_numf1(idx, sl, dcm, x1src, xnTsrc, mmat, use_prev, out_rows,
                       v_rden, v_bw)
            if i + 1 < n:
                ti[i + 1] = tile_gw(idx + 1, st[i + 1][0])
            pt_seq[0] += 1

    sq_fc = {}
    rstd0 = None
    for c in range(NCH):
        xcur = xa if c % 2 == 0 else xb
        xnext = xb if c % 2 == 0 else xa
        if c + 1 < NCH:
            E["sync"].wait(sem_st, R.val("st"))
            if c - 1 in carry_v:
                E["sync"].wait(sem_v, carry_v[c - 1])
            par = (c + 1) % 2
            E["sync"].dma_start(
                out=xnext[:],
                in_=_r(x_in[(c + 1) * T:(c + 2) * T, :], "(j p) e -> p j e", p=128),
                _inc=(sem_x[par], 16))
            x_ld[c + 1] = (par, R.bump("x%d" % par if par == 0 else "x1s", 16))

        if c == 0:
            E["scalar"].wait(sem_x[x_ld[c][0]], x_ld[c][1])
            E["scalar"].wait(sem_v, R.val("v"))
            E["vector"].wait(sem_x[x_ld[c][0]], x_ld[c][1])
            for j in range(NT):
                ln_sq_tile(xcur, j, 0)
            rstd0 = ln_finish(0)
            xn0_ap = ln_apply(xcur, 0, xn0T, rstd0)
        E["tensor"].wait(sem_v, xn0_ap)

        def kvr(w_int, mode, dst):
            cv = None if c > 0 else w_int
            srcap = w_int if c > 0 else cvt_src[w_int.tensor.name]
            sl_, ldt, wv = ring_load(srcap, DB, D, cvt=cv)
            E["tensor"].wait(SEMMAP[ldt[0]], ldt[1])
            evac = {}
            for eb in range(DB):
                key = "pa" if eb % 2 == 0 else "pb"
                reg = (pa if eb % 2 == 0 else pb)[:, 0:T]
                psum_use(key)
                for i in range(DB):
                    kw = {} if i < DB - 1 else {"_inc": (sem_t, 1)}
                    E["tensor"].matmul(reg, wv[:, i, eb * 128:(eb + 1) * 128],
                                       xn0T[:, i, :], start=(i == 0),
                                       stop=(i == DB - 1), **kw)
                tmm = R.bump("t")
                if mode == "s_copy":
                    E["scalar"].wait(sem_t, tmm)
                    E["scalar"].copy(dst[:, eb, :], reg, _inc=(sem_a, 1))
                    evac[eb] = R.bump("a")
                    psum_set(key, sem_a, evac[eb])
                elif mode == "s_tanh":
                    E["scalar"].wait(sem_t, tmm)
                    E["scalar"].activation(dst[:, eb, :], reg, AF.Tanh, scale=0.5,
                                           _inc=(sem_a, 1))
                    evac[eb] = R.bump("a")
                    psum_set(key, sem_a, evac[eb])
                else:
                    E["vector"].wait(sem_t, tmm)
                    E["vector"].tensor_copy(dst[:, eb, :], reg, _inc=(sem_v, 1))
                    evac[eb] = R.bump("v")
                    psum_set(key, sem_v, evac[eb])
            ring_release(sl_)
            return evac[DB - 1]

        k_done = kvr(wk_i, "s_copy", kbf)
        v_done = kvr(wv_i, "v_copy", vbf)
        r_done = kvr(wr_i, "s_tanh", trs)
        E["vector"].wait(sem_v, v_done)

        E["vector"].wait(sem_st, R.val("st"))
        E["vector"].wait(sem_a, k_done)
        E["vector"].tensor_mul(kbf[:], kbf[:], vbf[:], _inc=(sem_v, 1))
        v_kv = R.bump("v")
        E["vector"].wait(sem_v, v_kv)
        for i in range(DB):
            kw = {} if i < DB - 1 else {"_inc": (sem_v, 1)}
            E["vector"].tensor_tensor_scan(
                zs[:, i, 0:T], dec[:, i:i + 1].broadcast_to([128, T]),
                kbf[:, i, :], zcarry[:, i, 0:1], ALU.mult, ALU.add, **kw)
        v_scan = R.bump("v")
        E["vector"].wait(sem_v, v_scan)
        for i in range(DB):
            E["vector"].scalar_tensor_tensor(
                vbf[:, i, 0:1], kbf[:, i, 0:1], eu[:, i:i + 1], zcarry[:, i, 0:1],
                ALU.mult, ALU.add)
            kw = {} if i < DB - 1 else {"_inc": (sem_v, 1)}
            E["vector"].scalar_tensor_tensor(
                vbf[:, i, 1:T], kbf[:, i, 1:T], eu[:, i:i + 1], zs[:, i, 0:T - 1],
                ALU.mult, ALU.add, **kw)
        v_wkv = R.bump("v")
        E["vector"].wait(sem_v, v_wkv)
        E["vector"].tensor_copy(zcarry[:], zs[:, :, T - 1:T])
        E["vector"].wait(sem_a, r_done)
        E["vector"].scalar_tensor_tensor(trs[:], trs[:], 1.0, vbf[:], ALU.add,
                                         ALU.mult, _inc=(sem_v, 1))
        s0_done = R.bump("v")

        if 0 < c < NCH - 1:
            E["scalar"].wait(sem_x[x_ld[c + 1][0]], x_ld[c + 1][1])
            E["vector"].wait(sem_x[x_ld[c + 1][0]], x_ld[c + 1][1])
            buf_use("scalar", "xnt_sq")
            sqh = None
            for j in range(NT):
                if j > 0:
                    E["scalar"].wait(sem_a, sqh)
                E["scalar"].activation(xnt[:], xnext[:, j, :], AF.Square,
                                       accum_out=stats[:, 8 * j + 1:8 * j + 2],
                                       _inc=(sem_a, 1))
                sqh = R.bump("a")
                E["vector"].reduce_sum(stats[:, 8 * j:8 * j + 1], xnext[:, j, :],
                                       axis=AX, _inc=(sem_v, 1))
                R.bump("v")
            buf_set("xnt_sq", sem_a, sqh)
        E["vector"].wait(sem_st, R.val("st"))
        for q in range(2):
            colr = slice(q * HID // 2, (q + 1) * HID // 2)
            sl_u, ldt, wuv = ring_load(
                wu_i[:, :, colr] if c > 0 else cvt_src[wu_i.tensor.name][:, :, colr],
                DB, HID // 2, cvt=None if c > 0 else wu_i[:, :, colr])
            E["tensor"].wait(SEMMAP[ldt[0]], ldt[1])
            upev = {}
            for hb in range(HQ):
                key = "pa" if hb % 2 == 0 else "pb"
                reg = (pa if hb % 2 == 0 else pb)[:, 0:T]
                psum_use(key)
                for i in range(DB):
                    kw = {} if i < DB - 1 else {"_inc": (sem_t, 1)}
                    E["tensor"].matmul(reg, wuv[:, i, hb * 128:(hb + 1) * 128],
                                       xn0T[:, i, :], start=(i == 0),
                                       stop=(i == DB - 1), **kw)
                tmm = R.bump("t")
                if hb == 0 and q >= 1:
                    E["vector"].wait(sem_v, buf_guard.get("sgt_q", (sem_v, 0))[1])
                E["vector"].wait(sem_t, tmm)
                E["vector"].tensor_copy(up_sb[:, hb, :], reg, _inc=(sem_v, 1))
                upev[hb] = R.bump("v")
                psum_set(key, sem_v, upev[hb])
            ring_release(sl_u)
            sl_g, ldt, wgv = ring_load(
                wg_i[:, :, colr] if c > 0 else cvt_src[wg_i.tensor.name][:, :, colr],
                DB, HID // 2, cvt=None if c > 0 else wg_i[:, :, colr])
            E["tensor"].wait(SEMMAP[ldt[0]], ldt[1])
            sgc = {}
            hdone = {}
            for hb in range(HQ):
                key = "pa" if hb % 2 == 0 else "pb"
                reg = (pa if hb % 2 == 0 else pb)[:, 0:T]
                psum_use(key)
                for i in range(DB):
                    kw = {} if i < DB - 1 else {"_inc": (sem_t, 1)}
                    E["tensor"].matmul(reg, wgv[:, i, hb * 128:(hb + 1) * 128],
                                       xn0T[:, i, :], start=(i == 0),
                                       stop=(i == DB - 1), **kw)
                tmm = R.bump("t")
                if hb >= 1:
                    E["scalar"].wait(sem_v, hdone[hb - 1])
                elif q >= 1:
                    E["scalar"].wait(sem_v, buf_guard.get("sgt_q", (sem_v, 0))[1])
                E["scalar"].wait(sem_t, tmm)
                E["scalar"].activation(sgt, reg, AF.Silu, scale=1.0 / 127.0,
                                       _inc=(sem_a, 1))
                sgc[hb] = R.bump("a")
                psum_set(key, sem_a, sgc[hb])
                E["vector"].wait(sem_a, sgc[hb])
                E["vector"].tensor_mul(h_sb[:, q * HQ + hb, :], sgt, up_sb[:, hb, :],
                                       _inc=(sem_v, 1))
                hdone[hb] = R.bump("v")
            buf_set("sgt_q", sem_v, hdone[HQ - 1])
            ring_release(sl_g)
        h_all = R.val("v")

        sl_o, ldt, wov = ring_load(wo_i if c > 0 else cvt_src[wo_i.tensor.name],
                                   DB, D, cvt=None if c > 0 else wo_i)
        E["tensor"].wait(SEMMAP[ldt[0]], ldt[1])
        E["tensor"].wait(sem_v, s0_done)
        E["tensor"].wait(sem_v, h_all)
        s_evac = {}
        for j in range(NT):
            key = "pa" if j % 2 == 0 else "pb"
            reg = (pa if j % 2 == 0 else pb)[:, 0:1024]
            psum_use(key)
            for i in range(DB):
                for h2 in range(2):
                    kw = {} if not (i == DB - 1 and h2 == 1) else {"_inc": (sem_t, 1)}
                    E["tensor"].matmul(reg[:, h2 * 512:(h2 + 1) * 512],
                                       trs[:, i, j * 128:(j + 1) * 128],
                                       wov[:, i, h2 * 512:(h2 + 1) * 512],
                                       start=(i == 0), stop=(i == DB - 1), **kw)
            tmm = R.bump("t")
            E["vector"].wait(sem_t, tmm)
            E["vector"].tensor_copy(s_bf[:, j, :], reg, _inc=(sem_v, 1))
            s_evac[j] = R.bump("v")
            psum_set(key, sem_v, s_evac[j])
        ring_release(sl_o)

        sl_d0, ldt0, wdv0 = ring_load(
            wd_i[:, 0:HQ, :] if c > 0 else cvt_src[wd_i.tensor.name][:, 0:HQ, :],
            HQ, D, cvt=None if c > 0 else wd_i[:, 0:HQ, :])
        sl_d1, ldt1, wdv1 = ring_load(
            wd_i[:, HQ:HB, :] if c > 0 else cvt_src[wd_i.tensor.name][:, HQ:HB, :],
            HQ, D, cvt=None if c > 0 else wd_i[:, HQ:HB, :])
        E["tensor"].wait(SEMMAP[ldt0[0]], ldt0[1])
        E["tensor"].wait(SEMMAP[ldt1[0]], ldt1[1])
        E["tensor"].wait(sem_v, h_all)
        pregs = [("pa", pa), ("pb", pb), ("pc", pc)]
        for j in range(NT):
            key, reg = pregs[j % 3]
            psum_use(key)
            for hb in range(HB):
                wdv = wdv0 if hb < HQ else wdv1
                hbl = hb % HQ
                for h2 in range(2):
                    last = (hb == HB - 1 and h2 == 1)
                    kw = {} if not last else {"_inc": (sem_t, 1)}
                    E["tensor"].matmul(reg[:, h2 * 512:(h2 + 1) * 512],
                                       h_sb[:, hb, j * 128:(j + 1) * 128],
                                       wdv[:, hbl, h2 * 512:(h2 + 1) * 512],
                                       start=(hb == 0), stop=(hb == HB - 1), **kw)
            t_down = R.bump("t")
            g0t_j, t1t_j = g0ts[j % 2], t1ts[j % 2]
            E["scalar"].wait(sem_t, t_down)
            buf_use("scalar", "g0t%d" % (j % 2))
            E["scalar"].activation(g0t_j, reg[:], AF.Silu, scale=1.0 / 16129.0,
                                   _inc=(sem_a, 1))
            ag0 = R.bump("a")
            psum_set(key, sem_a, ag0)
            E["gpsimd"].wait(sem_a, ag0)
            E["gpsimd"].wait(sem_v, s_evac[j])
            buf_use("gpsimd", "t1t%d" % (j % 2))
            E["gpsimd"].tensor_mul(t1t_j, s_bf[:, j, :], g0t_j, _inc=(sem_gc, 1))
            gmul = R.bump("gc")
            buf_set("g0t%d" % (j % 2), sem_gc, gmul)
            E["vector"].wait(sem_gc, gmul)
            E["vector"].scalar_tensor_tensor(xcur[:, j, :], t1t_j, 0.5,
                                             xcur[:, j, :], ALU.mult, ALU.add,
                                             _inc=(sem_v, 1))
            vx1 = R.bump("v")
            buf_set("t1t%d" % (j % 2), sem_v, vx1)
            E["scalar"].wait(sem_v, vx1)
            E["vector"].wait(sem_v, vx1)
            ln_sq_tile(xcur, j, 32)
            if c > 0:
                if j == 0:
                    a_pre0 = tile_pre(deprev[:], 0, v_need=carry_v[c - 1])
                elif j == 1:
                    v_st0 = stats_batch(1, 0, a_need=a_pre0)
        ring_release(sl_d0)
        ring_release(sl_d1)

        if c > 0:
            v_rd0 = dens_batch(1, v_st0, False, 0)
            t_d0 = tile_dots_mm(0, deprev[:])
            st0 = tile_exp(0, t_d0, v_st0, barrier_t=True)
            ti0 = tile_gw(0, st0[0])
            sl0 = pt_seq[0] % 2
            v_bw0 = tile_t2blend(0, sl0, ti0, st0[1], gc_need=R.val("gc"))
            tile_numf1(0, sl0, deprev[:], x1prev[:], xn1Tprev[:], mbandb, True,
                       (c - 1) * T + (NT - 1) * 128, v_rd0, v_bw0)
            pt_seq[0] += 1

        if c == 0:
            emit_l1_prep()

        rstd1 = ln_finish(32)
        if 0 < c < NCH - 1:
            rstd0 = ln_finish(0)
        xn1T_done = ln_apply(xcur, 32, xn1T, rstd1)
        E["vector"].wait(sem_v, xn1T_done)
        if c > 0:
            E["vector"].tensor_sub(deprev[:, :, 127:128], xn1T[:, :, 0:1],
                                   xn1last[:, :, 0:1])
        buf_use("vector", "de")
        E["vector"].tensor_sub(de[:, :, 0:T - 1], xn1T[:, :, 1:T],
                               xn1T[:, :, 0:T - 1])
        E["vector"].memset(de[:, :, T - 1:T], 0.0, _inc=(sem_v, 1))
        R.bump("v")

        idx0 = 1 if c > 0 else 0
        tiles = []
        for jt in range(NT - 1):
            mm = mfirstb if (c == 0 and jt == 0) else mbandb
            useprev = not (c == 0 and jt == 0)
            tiles.append((de[:, :, jt * 128:(jt + 1) * 128], xcur[:, jt, :],
                          xn1T[:, :, jt * 128:(jt + 1) * 128], mm, useprev,
                          c * T + jt * 128))
        for i, tl in enumerate(tiles):
            tile_pre(tl[0], idx0 + i)
        v_stats = stats_batch(len(tiles), idx0)
        v_rden = dens_batch(len(tiles), v_stats, c == 0, idx0)
        tile_pipeline(tiles, v_stats, v_rden, idx0)

        if c == 0:
            xnx = xb if c % 2 == 0 else xa
            E["scalar"].wait(sem_x[x_ld[c + 1][0]], x_ld[c + 1][1])
            E["vector"].wait(sem_x[x_ld[c + 1][0]], x_ld[c + 1][1])
            for j in range(NT):
                ln_sq_tile(xnx, j, 0)
            rstd0 = ln_finish(0)
        if c + 1 < NCH:
            xn0_ap = ln_apply(xnext, 0, xn0T, rstd0)
        E["vector"].wait(sem_v, R.val("v"))
        E["vector"].tensor_copy(mwb4[:, 0:1], mwb4[:, idx0 + 3:idx0 + 4],
                                _inc=(sem_v, 1))
        R.bump("v")

        tlast = R.val("t")
        E["vector"].wait(sem_t, tlast)
        E["vector"].tensor_copy(deprev[:], de[:, :, (NT - 1) * 128:T])
        E["vector"].tensor_copy(xn1last[:], xn1T[:, :, T - 1:T])
        E["vector"].tensor_copy(xn1Tprev[:], xn1T[:, :, (NT - 1) * 128:T])
        E["vector"].tensor_copy(x1prev[:], xcur[:, NT - 1, :], _inc=(sem_v, 1))
        v_carry = R.bump("v")
        carry_v[c] = v_carry
        buf_set("de", sem_v, v_carry)

    a_pre0 = tile_pre(deprev[:], 0, v_need=carry_v[NCH - 1])
    v_st0 = stats_batch(1, 0, a_need=a_pre0)
    v_rd0 = dens_batch(1, v_st0, False, 0)
    st0 = tile_dots(0, deprev[:], v_st0)
    ti0 = tile_gw(0, st0[0])
    sl0 = pt_seq[0] % 2
    v_bw0 = tile_t2blend(0, sl0, ti0, st0[1])
    tile_numf1(0, sl0, deprev[:], x1prev[:], xn1Tprev[:], mbandb, True,
               (NCH - 1) * T + (NT - 1) * 128, v_rd0, v_bw0)
    pt_seq[0] += 1
    E["sync"].wait(sem_st, R.val("st"))

    with es:
        block = es.enter_context(nc.Block())
        block.sync(lambda e: [fn(e) for fn in R.prog["sync"]])
        block.gpsimd(lambda e: [fn(e) for fn in R.prog["gpsimd"]])
        block.tensor(lambda e: [fn(e) for fn in R.prog["tensor"]])
        block.vector(lambda e: [fn(e) for fn in R.prog["vector"]])
        block.scalar(lambda e: [fn(e) for fn in R.prog["scalar"]])
    return nc


def host_prep(inputs):
    f32 = np.float32
    Mband, Mprev, Mfirst = make_M_matrices()
    M64b, M64p, M64f = make_M_matrices_f64()
    Mlo = np.concatenate([(M64b - Mband.astype(np.float64)).astype(f32),
                          (M64p - Mprev.astype(np.float64)).astype(f32),
                          (M64f - Mfirst.astype(np.float64)).astype(f32)], axis=1)
    return {
        "wkT": np.ascontiguousarray(np.asarray(inputs["L0_attn_Wk"], f32).T),
        "wvT": np.ascontiguousarray(np.asarray(inputs["L0_attn_Wv"], f32).T),
        "wrT": np.ascontiguousarray(np.asarray(inputs["L0_attn_Wr"], f32).T),
        "woT": np.ascontiguousarray(np.asarray(inputs["L0_attn_Wo"], f32).T),
        "wgT": np.ascontiguousarray(np.asarray(inputs["L0_ffn_w_gate"], np.int32).T),
        "wuT": np.ascontiguousarray(np.asarray(inputs["L0_ffn_w_up"], np.int32).T),
        "wdT": np.ascontiguousarray(np.asarray(inputs["L0_ffn_w_down"], np.int32).T),
        "td_c": np.ascontiguousarray(np.asarray(inputs["L0_attn_time_decay"], f32).reshape(DB, 128).T),
        "tf_c": np.ascontiguousarray(np.asarray(inputs["L0_attn_time_first"], f32).reshape(DB, 128).T),
        "guid": np.ascontiguousarray(np.asarray(inputs["L1_attn_guidance_trajectories"], f32)),
        "wlT": np.ascontiguousarray(np.asarray(inputs["L1_ffn_weight"], f32).T),
        "ident": np.eye(128, dtype=ml_dtypes.bfloat16),
        "cvec_in": np.broadcast_to(np.array([1e-5, np.log(2.0), 0.0, 0.0], f32), (128, 4)).copy(),
        "mband": Mband, "mprev": Mprev, "mfirst": Mfirst,
        "mlo": Mlo,
        "onesc": np.ones((128, 1), f32), "onesr": np.ones((1, 128), f32),
    }


_NC_CACHE = {}


def kernel(**inputs):
    x = np.asarray(inputs["L0_attn_x"], np.float32)
    B = x.shape[0]
    if "nc" not in _NC_CACHE:
        _NC_CACHE["nc"] = build_fused()
    nc = _NC_CACHE["nc"]
    shared = host_prep(inputs)
    in_maps = []
    for b in range(B):
        m = dict(shared)
        m["x"] = np.ascontiguousarray(x[b])
        in_maps.append(m)
    res = run_bass_kernel_spmd(nc, in_maps, core_ids=list(range(B)))
    return np.stack([r["out"] for r in res.results], axis=0)









# version 22
# speedup vs baseline: 1.0313x; 1.0313x over previous
import contextlib
import numpy as np
import ml_dtypes
import concourse.bass as bass
import concourse.mybir as mybir
from concourse.bass_utils import run_bass_kernel_spmd

F32 = mybir.dt.float32
BF16 = mybir.dt.bfloat16
I32 = mybir.dt.int32
AF = mybir.ActivationFunctionType
ALU = mybir.AluOpType
AX = mybir.AxisListType.X

S, D, HID, G = 4096, 1024, 3072, 512
DB = D // 128
HB = HID // 128
GB = G // 128

ENGS = ("sync", "gpsimd", "tensor", "vector", "scalar")


class Rec:
    def __init__(self):
        self.prog = {k: [] for k in ENGS}
        self.cnt = {}

    def eng(self, name):
        return _EngProxy(self, name)

    def bump(self, sem_name, n=1):
        self.cnt[sem_name] = self.cnt.get(sem_name, 0) + n
        return self.cnt[sem_name]

    def val(self, sem_name):
        return self.cnt.get(sem_name, 0)


class _EngProxy:
    def __init__(self, rec, name):
        self._rec = rec
        self._name = name

    def wait(self, sem, val):
        if val > 0:
            self._rec.prog[self._name].append(lambda e: e.wait_ge(sem, val))

    def __getattr__(self, attr):
        rec, name = self._rec, self._name

        def call(*args, _inc=None, **kwargs):
            def fn(eng, attr=attr, args=args, kwargs=kwargs, _inc=_inc):
                ins = getattr(eng, attr)(*args, **kwargs)
                if _inc is not None:
                    ins.then_inc(*_inc)
            rec.prog[name].append(fn)
        return call


def _r(t, pat, **kw):
    return t.rearrange(pat, **kw)


def make_M_matrices():
    W = 8

    def pw(s, k):
        idx = s - W + k
        if idx < 0 or s < 1:
            return 0.0
        w_cnt = min(s, W)
        j = k - (W - w_cnt)
        if w_cnt > 1:
            return 0.1 + 0.9 * j / (w_cnt - 1)
        return 0.1

    Mband = np.zeros((128, 128), np.float64)
    Mprev = np.zeros((128, 128), np.float64)
    Mfirst = np.zeros((128, 128), np.float64)
    for sl in range(128):
        for k in range(W):
            jl = sl - W + k
            if jl >= 0:
                Mband[jl, sl] = pw(128 + sl, k)
                Mfirst[jl, sl] = pw(sl, k)
            else:
                Mprev[128 + jl, sl] = pw(128 + sl, k)
    return (Mband.astype(np.float32), Mprev.astype(np.float32),
            Mfirst.astype(np.float32))


def make_M_matrices_f64():
    W = 8

    def pw(s, k):
        idx = s - W + k
        if idx < 0 or s < 1:
            return 0.0
        w_cnt = min(s, W)
        j = k - (W - w_cnt)
        if w_cnt > 1:
            return 0.1 + 0.9 * j / (w_cnt - 1)
        return 0.1

    Mband = np.zeros((128, 128), np.float64)
    Mprev = np.zeros((128, 128), np.float64)
    Mfirst = np.zeros((128, 128), np.float64)
    for sl in range(128):
        for k in range(W):
            jl = sl - W + k
            if jl >= 0:
                Mband[jl, sl] = pw(128 + sl, k)
                Mfirst[jl, sl] = pw(sl, k)
            else:
                Mprev[128 + jl, sl] = pw(128 + sl, k)
    return Mband, Mprev, Mfirst


def build_fused(T=512):
    NT = T // 128
    NCH = S // T
    HQ = 12
    nc = bass.Bass("TRN2", target_bir_lowering=False, debug=False)
    x_in = nc.dram_tensor("x", [S, D], F32, kind="ExternalInput").ap()
    wkT = nc.dram_tensor("wkT", [D, D], F32, kind="ExternalInput").ap()
    wvT = nc.dram_tensor("wvT", [D, D], F32, kind="ExternalInput").ap()
    wrT = nc.dram_tensor("wrT", [D, D], F32, kind="ExternalInput").ap()
    woT = nc.dram_tensor("woT", [D, D], F32, kind="ExternalInput").ap()
    wgT = nc.dram_tensor("wgT", [D, HID], I32, kind="ExternalInput").ap()
    wuT = nc.dram_tensor("wuT", [D, HID], I32, kind="ExternalInput").ap()
    wdT = nc.dram_tensor("wdT", [HID, D], I32, kind="ExternalInput").ap()
    td_c = nc.dram_tensor("td_c", [128, DB], F32, kind="ExternalInput").ap()
    tf_c = nc.dram_tensor("tf_c", [128, DB], F32, kind="ExternalInput").ap()
    guid_d = nc.dram_tensor("guid", [G, D], F32, kind="ExternalInput").ap()
    wlT_d = nc.dram_tensor("wlT", [D, D], F32, kind="ExternalInput").ap()
    ident_d = nc.dram_tensor("ident", [128, 128], BF16, kind="ExternalInput").ap()
    cvec_d = nc.dram_tensor("cvec_in", [128, 4], F32, kind="ExternalInput").ap()
    mband_d = nc.dram_tensor("mband", [128, 128], F32, kind="ExternalInput").ap()
    mprev_d = nc.dram_tensor("mprev", [128, 128], F32, kind="ExternalInput").ap()
    mfirst_d = nc.dram_tensor("mfirst", [128, 128], F32, kind="ExternalInput").ap()
    mlo_d = nc.dram_tensor("mlo", [128, 384], F32, kind="ExternalInput").ap()
    onesc_d = nc.dram_tensor("onesc", [128, 1], F32, kind="ExternalInput").ap()
    onesr_d = nc.dram_tensor("onesr", [1, 128], F32, kind="ExternalInput").ap()
    out_d = nc.dram_tensor("out", [S, D], F32, kind="ExternalOutput").ap()
    wk_i = nc.dram_tensor("wk_i", [128, DB, D], BF16, kind="Internal").ap()
    wv_i = nc.dram_tensor("wv_i", [128, DB, D], BF16, kind="Internal").ap()
    wr_i = nc.dram_tensor("wr_i", [128, DB, D], BF16, kind="Internal").ap()
    wo_i = nc.dram_tensor("wo_i", [128, DB, D], BF16, kind="Internal").ap()
    wg_i = nc.dram_tensor("wg_i", [128, DB, HID], BF16, kind="Internal").ap()
    wu_i = nc.dram_tensor("wu_i", [128, DB, HID], BF16, kind="Internal").ap()
    wd_i = nc.dram_tensor("wd_i", [128, HB, D], BF16, kind="Internal").ap()

    es = contextlib.ExitStack()
    sbt = lambda name, shape, dt: es.enter_context(nc.sbuf_tensor(name, shape, dt))
    ring = [sbt("ring0", [128, 12288], BF16), sbt("ring1", [128, 12288], BF16)]
    argA1 = sbt("argA1", [128, 4096], F32)
    argA2 = sbt("argA2", [128, 12288], BF16)
    argB = sbt("argB", [128, 6144], BF16)
    argC = sbt("argC", [128, 4096], BF16)
    xa = sbt("xa", [128, NT, D], F32)
    xb = sbt("xb", [128, NT, D], F32)
    xn0T = sbt("xn0T", [128, DB, T], BF16)
    trs = sbt("trs", [128, DB, T], BF16)
    zcarry = sbt("zcarry", [128, DB, 1], F32)
    zs = _r(argA1[:, 0:DB * T], "p (i t) -> p i t", i=DB)
    stg2 = _r(argA1[:, 0:4 * D], "p (g e) -> p g e", g=4)
    tr1 = argA1[:, 0:D]
    tg = argA1[:, D:2 * D]
    x2t = tg
    mstg = argA1[:, 2 * D:2 * D + 384]
    mlo_s = argA1[:, 2 * D + 384:2 * D + 768]
    h_sb = _r(argA2[:, 0:HB * T], "p (i t) -> p i t", i=HB)
    gnrm = argA2[:, 0:D]
    de = _r(argA2[:, 0:DB * T], "p (i t) -> p i t", i=DB)
    xn1T = _r(argA2[:, DB * T:2 * DB * T], "p (i t) -> p i t", i=DB)
    difs = [argB[:, 5120:5120 + D], argA2[:, 2 * DB * T:2 * DB * T + D],
            argB[:, 4096:4096 + D], argB[:, 0:D]]
    g1t = argA2[:, 2 * DB * T + D:2 * DB * T + 2 * D]
    gw = argA2[:, 2 * DB * T + 2 * D:2 * DB * T + 2 * D + G]
    gwTv = argA2[:, 2 * DB * T + 2 * D + G:2 * DB * T + 2 * D + 2 * G]
    exp_sb = argA2[:, 2 * DB * T + 2 * D + 2 * G:2 * DB * T + 2 * D + 3 * G]
    kbf = _r(argB[:, 0:DB * T], "p (i t) -> p i t", i=DB)
    s_bf = _r(argB[:, 0:NT * D], "p (j e) -> p j e", j=NT)
    up_sb = _r(argB[:, 0:HQ * T], "p (i t) -> p i t", i=HQ)
    sq4 = _r(argB[:, 0:NT * D], "p (j e) -> p j e", j=NT)
    vbf = _r(argC[:, 0:DB * T], "p (i t) -> p i t", i=DB)
    t2 = argC[:, 2560:2560 + D]
    sgt = argC[:, 0:T]
    g0ts = [argC[:, T:T + D], argC[:, 2560:2560 + D]]
    t1ts = [argC[:, T + D:T + 2 * D], argB[:, 4096:4096 + D]]
    xnt = sbt("xnt", [128, D], BF16)
    stats = sbt("stats", [128, 64], F32)
    statL = sbt("statL", [128, 40], F32)
    mwb4 = sbt("mwb4", [128, 8], BF16)
    tdc = sbt("tdc", [128, DB], F32)
    tfc = sbt("tfc", [128, DB], F32)
    dec = sbt("dec", [128, DB], F32)
    eu = sbt("eu", [128, DB], F32)
    deprev = sbt("deprev", [128, DB, 128], BF16)
    xn1Tprev = sbt("xn1Tprev", [128, DB, 128], BF16)
    xn1last = sbt("xn1last", [128, DB, 1], BF16)
    x1prev = sbt("x1prev", [128, D], F32)
    gwT = _r(gwTv, "p (i t) -> p i t", i=GB)
    bw = [sbt("bw0", [128, D], BF16), sbt("bw1", [128, D], BF16)]
    mwb = sbt("mwb", [128, 2], BF16)
    guid_nT = sbt("guid_nT", [128, DB, G], BF16)
    guidb = sbt("guidb", [128, GB, D], BF16)
    wl_tern = sbt("wl_tern", [128, DB, D], BF16)
    gn2 = sbt("gn2", [128, GB], F32)
    invg = sbt("invg", [128, GB], F32)
    aw = sbt("aw", [128, DB], F32)
    scl = sbt("scl", [1, 4], F32)
    scaleB = sbt("scaleB", [128, 1], F32)
    rscB = sbt("rscB", [128, 1], F32)
    identb = sbt("identb", [128, 128], BF16)
    cvec = sbt("cvec", [128, 4], F32)
    mbandb = sbt("mbandb", [128, 128], BF16)
    mprevb = sbt("mprevb", [128, 128], BF16)
    mfirstb = sbt("mfirstb", [128, 128], BF16)

    onescb = sbt("onescb", [128, 1], F32)
    onesrb = sbt("onesrb", [1, 128], F32)

    psm = lambda name, shape, dt: es.enter_context(nc.psum_tensor(name, shape, dt))
    pa = psm("pa", [128, 1024], F32)
    pb = psm("pb", [128, 1024], F32)
    pc = psm("pc", [128, 1024], F32)
    pd = psm("pd", [128, 512], F32)
    pt = psm("pt", [128, 1024], BF16)

    sem = {}
    for s_ in ("mi", "x0", "x1s", "w0", "w1", "g0", "g1", "gc", "gl", "c0", "c1", "t", "v", "a", "st"):
        sem[s_] = es.enter_context(nc.semaphore(s_))
    sem_mi, sem_t, sem_v, sem_a, sem_st, sem_gc = (
        sem["mi"], sem["t"], sem["v"], sem["a"], sem["st"], sem["gc"])
    sem_x = [sem["x0"], sem["x1s"]]
    sem_gl = sem["gl"]
    sem_c = [sem["c0"], sem["c1"]]
    sem_w = [sem["w0"], sem["w1"]]
    sem_g = [sem["g0"], sem["g1"]]

    R = Rec()
    E = {k: R.eng(k) for k in ENGS}

    psum_guard = {}

    def psum_use(key):
        g = psum_guard.get(key)
        if g is not None:
            E["tensor"].wait(g[0], g[1])

    def psum_set(key, sm, count):
        psum_guard[key] = (sm, count)

    buf_guard = {}

    def buf_use(eng, name):
        g = buf_guard.get(name)
        if g is not None:
            E[eng].wait(g[0], g[1])

    def buf_set(name, sm, count):
        buf_guard[name] = (sm, count)

    def vfence():
        E["vector"].wait(sem_v, R.val("v"))

    def afence():
        E["scalar"].wait(sem_a, R.val("a"))

    for (dst, srcp) in [(identb[:], ident_d), (cvec[:], cvec_d),
                        (mstg[:, 0:128], mband_d), (mstg[:, 128:256], mprev_d),
                        (mstg[:, 256:384], mfirst_d), (onescb[:], onesc_d),
                        (onesrb[:], onesr_d), (tdc[:], td_c), (tfc[:], tf_c),
                        (mlo_s, mlo_d)]:
        E["sync"].dma_start(out=dst, in_=srcp, _inc=(sem_mi, 16)); R.bump("mi", 16)
    mi_done0 = R.val("mi")
    E["vector"].wait(sem_mi, mi_done0)
    E["vector"].tensor_add(mstg, mstg, mlo_s, _inc=(sem_v, 1)); R.bump("v")
    E["vector"].wait(sem_v, R.val("v"))
    E["vector"].tensor_copy(mbandb[:], mstg[:, 0:128])
    E["vector"].tensor_copy(mprevb[:], mstg[:, 128:256])
    E["vector"].tensor_copy(mfirstb[:], mstg[:, 256:384], _inc=(sem_v, 1))
    v_mdone = R.bump("v")
    mi_done = R.val("mi")
    E["sync"].dma_start(out=xa[:], in_=_r(x_in[0:T, :], "(j p) e -> p j e", p=128),
                        _inc=(sem_x[0], 16))
    x_ld = {0: (0, R.bump("x0", 16))}

    E["vector"].wait(sem_mi, mi_done)
    E["vector"].memset(zcarry[:], 0.0, _inc=(sem_v, 1)); R.bump("v")
    E["scalar"].wait(sem_mi, mi_done)
    E["scalar"].activation(dec[:], tdc[:], AF.Exp, _inc=(sem_a, 1)); R.bump("a")
    afence()
    E["scalar"].activation(dec[:], dec[:], AF.Exp, scale=-1.0)
    E["scalar"].activation(eu[:], tfc[:], AF.Exp, _inc=(sem_a, 1)); R.bump("a")

    cvt_src = {}
    for (srcp, dst) in [(wkT, wk_i), (wvT, wv_i), (wrT, wr_i), (woT, wo_i),
                        (wgT, wg_i), (wuT, wu_i)]:
        cvt_src[dst.tensor.name] = _r(srcp, "(i p) e -> p i e", p=128)
    cvt_src[wd_i.tensor.name] = _r(wdT, "(i p) e -> p i e", p=128)
    slot = 0
    piece_store = {}

    free_at = {}
    state = {"slot": slot}
    SEMMAP = {"st": sem_st, "t": sem_t, "g0": sem_g[0], "g1": sem_g[1],
              "w0": sem_w[0], "w1": sem_w[1], "c0": sem_c[0], "c1": sem_c[1]}

    def _wait_free(eng, sl):
        for (s_, v_) in free_at.get(sl - 2, ()):
            E[eng].wait(SEMMAP[s_], v_)

    def ring_load(src_ap, nb, cols, cvt=None):
        sl = state["slot"]; state["slot"] += 1
        par = sl % 2
        v = ring[par][:, 0:nb * cols].rearrange("p (i e) -> p i e", i=nb)
        if cvt is None:
            _wait_free("sync", sl)
            E["sync"].dma_start(out=v, in_=src_ap, _inc=(sem_w[par], 16))
            ldt = ("w%d" % par, R.bump("w%d" % par, 16))
            free_at[sl] = []
        else:
            _wait_free("gpsimd", sl)
            E["gpsimd"].dma_start(out=v, in_=src_ap, _inc=(sem_g[par], 16))
            ldt = ("g%d" % par, R.bump("g%d" % par, 16))
            E["sync"].wait(sem_g[par], ldt[1])
            E["sync"].dma_start(out=cvt, in_=v, _inc=(sem_c[par], 16))
            free_at[sl] = [("c%d" % par, R.bump("c%d" % par, 16))]
        return sl, ldt, v

    def ring_release(sl):
        free_at[sl] = free_at.get(sl, []) + [("t", R.val("t"))]

    def emit_l1_prep():
        E["scalar"].wait(sem_a, R.val("a"))
        E["gpsimd"].wait(sem_v, v_mdone)
        E["gpsimd"].dma_start(out=stg2[:], in_=_r(guid_d, "(i p) e -> p i e", p=128),
                              _inc=(sem_gl, 16)); R.bump("gl", 16)
        gl_guid = R.val("gl")
        E["scalar"].wait(sem_gl, gl_guid)
        E["vector"].wait(sem_gl, gl_guid)
        for gi in range(GB):
            E["scalar"].activation(sq4[:, gi % NT, :], stg2[:, gi, :], AF.Square,
                                   accum_out=gn2[:, gi:gi + 1], _inc=(sem_a, 1))
            R.bump("a")
            if gi % NT == NT - 1:
                afence()
        a_gn2 = R.val("a")
        E["scalar"].wait(sem_a, a_gn2)
        E["scalar"].activation(gn2[:], gn2[:], AF.Ln, _inc=(sem_a, 1)); R.bump("a")
        afence()
        E["scalar"].activation(gn2[:], gn2[:], AF.Exp, scale=0.5, _inc=(sem_a, 1))
        a_gn = R.bump("a")
        E["vector"].wait(sem_a, a_gn)
        E["vector"].tensor_scalar_max(invg[:], gn2[:], 1e-8, _inc=(sem_v, 1)); R.bump("v")
        vfence()
        E["vector"].reciprocal(invg[:], invg[:], _inc=(sem_v, 1)); R.bump("v")
        vfence()
        for gi in range(GB):
            buf_use("vector", "gnrm")
            E["vector"].tensor_scalar(gnrm, stg2[:, gi, :], invg[:, gi:gi + 1],
                                      None, ALU.mult)
            E["vector"].tensor_copy(guidb[:, gi, :], stg2[:, gi, :], _inc=(sem_v, 1))
            v_gn = R.bump("v")
            E["tensor"].wait(sem_v, v_gn)
            psum_use("pt")
            for i in range(DB):
                kw = {} if i < DB - 1 else {"_inc": (sem_t, 1)}
                E["tensor"].transpose(pt[:, i * 128:(i + 1) * 128],
                                      gnrm[:, i * 128:(i + 1) * 128], identb[:], **kw)
            t_tp = R.bump("t")
            buf_set("gnrm", sem_t, t_tp)
            E["vector"].wait(sem_t, t_tp)
            E["vector"].tensor_copy(guid_nT[:, :, gi * 128:(gi + 1) * 128],
                                    _r(pt[:], "p (i t) -> p i t", i=DB), _inc=(sem_v, 1))
            psum_set("pt", sem_v, R.bump("v"))

        for hf in range(2):
            E["gpsimd"].wait(sem_v, R.val("v"))
            E["gpsimd"].wait(sem_a, R.val("a"))
            E["gpsimd"].dma_start(out=stg2[:],
                                in_=_r(wlT_d, "(i p) e -> p i e", p=128)[:, 4 * hf:4 * hf + 4, :],
                                _inc=(sem_gl, 16))
            mi_wl = R.bump("gl", 16)
            E["scalar"].wait(sem_gl, mi_wl)
            for q in range(4):
                i = 4 * hf + q
                E["scalar"].activation(sq4[:, q % NT, :], stg2[:, q, :], AF.Abs,
                                       accum_out=aw[:, i:i + 1], _inc=(sem_a, 1))
                R.bump("a")
                if q % NT == NT - 1:
                    afence()
        a_aw = R.val("a")
        E["vector"].wait(sem_a, a_aw)
        E["vector"].reduce_sum(gn2[:, 0:1], aw[:], axis=AX, _inc=(sem_v, 1))
        v_aw = R.bump("v")
        E["tensor"].wait(sem_v, v_aw)
        psum_use("pd")
        E["tensor"].matmul(pd[0:1, 0:1], onescb[:], gn2[:, 0:1], start=True, stop=True,
                           _inc=(sem_t, 1))
        t_tot = R.bump("t")
        E["vector"].wait(sem_t, t_tot)
        E["vector"].tensor_scalar(scl[0:1, 0:1], pd[0:1, 0:1], 1.0 / (D * D), None,
                                  ALU.mult, _inc=(sem_v, 1))
        psum_set("pd", sem_v, R.bump("v"))
        vfence()
        E["vector"].tensor_scalar_max(scl[0:1, 1:2], scl[0:1, 0:1], 1e-5, _inc=(sem_v, 1))
        R.bump("v")
        vfence()
        E["vector"].reciprocal(scl[0:1, 2:3], scl[0:1, 1:2], _inc=(sem_v, 1))
        v_scl = R.bump("v")
        E["tensor"].wait(sem_v, v_scl)
        psum_use("pd")
        E["tensor"].matmul(pd[:, 1:2], onesrb[:], scl[0:1, 1:2], start=True, stop=True)
        E["tensor"].matmul(pd[:, 2:3], onesrb[:], scl[0:1, 2:3], start=True, stop=True,
                           _inc=(sem_t, 1))
        t_bc = R.bump("t")
        E["vector"].wait(sem_t, t_bc)
        E["vector"].tensor_copy(scaleB[:], pd[:, 1:2])
        E["vector"].tensor_copy(rscB[:], pd[:, 2:3], _inc=(sem_v, 1))
        v_bc = R.bump("v")
        psum_set("pd", sem_v, v_bc)
        E["vector"].wait(sem_v, v_bc)
        for hf in range(2):
            E["gpsimd"].wait(sem_v, R.val("v"))
            E["gpsimd"].wait(sem_a, R.val("a"))
            E["gpsimd"].dma_start(out=stg2[:],
                                in_=_r(wlT_d, "(i p) e -> p i e", p=128)[:, 4 * hf:4 * hf + 4, :],
                                _inc=(sem_gl, 16))
            mi_wl = R.bump("gl", 16)
            E["vector"].wait(sem_gl, mi_wl)
            for q in range(4):
                i = 4 * hf + q
                zscr = _r(xn1T[:, 0:2, :], "p a t -> p (a t)")
                zascr = _r(xn1T[:, 2:4, :], "p a t -> p (a t)")
                E["vector"].tensor_scalar(zscr, stg2[:, q, :], rscB[:, 0:1], None,
                                          ALU.mult, _inc=(sem_v, 1))
                v_z = R.bump("v")
                E["scalar"].wait(sem_v, v_z)
                E["scalar"].activation(difs[0], zscr, AF.Sign)
                E["scalar"].activation(zascr, zscr, AF.Abs, _inc=(sem_a, 1))
                a_as = R.bump("a")
                E["vector"].wait(sem_a, a_as)
                E["vector"].tensor_single_scalar(g1t, zascr, 0.5, ALU.is_gt,
                                                 _inc=(sem_v, 1))
                R.bump("v")
                vfence()
                E["vector"].tensor_mul(wl_tern[:, i, :], g1t, difs[0], _inc=(sem_v, 1))
                R.bump("v")
                vfence()


    def ln_sq_tile(xcur, j, off):
        cs = off + 8 * j
        buf_use("scalar", "xnt_sq")
        E["scalar"].activation(xnt[:], xcur[:, j, :], AF.Square,
                               accum_out=stats[:, cs + 1:cs + 2], _inc=(sem_a, 1))
        a_ = R.bump("a")
        buf_set("xnt_sq", sem_a, a_)
        E["vector"].reduce_sum(stats[:, cs + 0:cs + 1], xcur[:, j, :], axis=AX,
                               _inc=(sem_v, 1))
        return a_, R.bump("v")

    def ln_finish(off):
        E["vector"].wait(sem_v, R.val("v"))
        E["vector"].wait(sem_a, R.val("a"))
        for j in range(NT):
            cs = off + 8 * j
            kw = {} if j < NT - 1 else {"_inc": (sem_v, 1)}
            E["vector"].tensor_scalar(stats[:, cs + 2:cs + 3], stats[:, cs + 0:cs + 1],
                                      -1.0 / D, None, ALU.mult, **kw)
        v_mneg = R.bump("v")
        E["vector"].wait(sem_v, v_mneg)
        for j in range(NT):
            cs = off + 8 * j
            kw = {} if j < NT - 1 else {"_inc": (sem_v, 1)}
            E["vector"].scalar_tensor_tensor(stats[:, cs + 3:cs + 4],
                                             stats[:, cs + 0:cs + 1],
                                             stats[:, cs + 2:cs + 3],
                                             stats[:, cs + 1:cs + 2],
                                             ALU.mult, ALU.add, **kw)
        v_var = R.bump("v")
        E["scalar"].wait(sem_v, v_var)
        var_cols = _r(stats[:, off:off + 8 * NT], "p (j k) -> p j k", j=NT)[:, :, 3:4]
        E["scalar"].activation(var_cols, var_cols, AF.Ln, scale=1.0 / D,
                               bias=cvec[:, 0:1], _inc=(sem_a, 1))
        a_ln = R.bump("a")
        E["scalar"].wait(sem_a, a_ln)
        E["scalar"].activation(var_cols, var_cols, AF.Exp, scale=-0.5, _inc=(sem_a, 1))
        return R.bump("a")

    def ln_apply(xcur, off, xnT, a_rstd):
        E["vector"].wait(sem_a, a_rstd)
        xnT_done = None
        for j in range(NT):
            cs = off + 8 * j
            E["vector"].tensor_scalar(xnt[:], xcur[:, j, :], stats[:, cs + 2:cs + 3],
                                      stats[:, cs + 3:cs + 4], ALU.add, ALU.mult,
                                      _inc=(sem_v, 1))
            v_apply = R.bump("v")
            E["tensor"].wait(sem_v, v_apply)
            psum_use("pt")
            for i in range(DB):
                kw = {} if i < DB - 1 else {"_inc": (sem_t, 1)}
                E["tensor"].transpose(pt[:, i * 128:(i + 1) * 128],
                                      xnt[:, i * 128:(i + 1) * 128], identb[:], **kw)
            t_tp = R.bump("t")
            E["vector"].wait(sem_t, t_tp)
            E["vector"].tensor_copy(xnT[:, :, j * 128:(j + 1) * 128],
                                    _r(pt[:], "p (i t) -> p i t", i=DB),
                                    _inc=(sem_v, 1))
            xnT_done = R.bump("v")
            psum_set("pt", sem_v, xnT_done)
        return xnT_done

    pt_seq = [0]
    carry_v = {}

    def tile_pre(dcm, idx, v_need=None):
        E["tensor"].wait(sem_v, v_need if v_need is not None else R.val("v"))
        psum_use("pt")
        buf_use("tensor", "dif%d" % idx)
        for i in range(DB):
            kw = {} if i < DB - 1 else {"_inc": (sem_t, 1)}
            E["tensor"].transpose(pt[:, i * 128:(i + 1) * 128], dcm[:, i, :],
                                  identb[:], **kw)
        t_tp = R.bump("t")
        buf_use("vector", "difv%d" % idx)
        E["vector"].wait(sem_t, t_tp)
        E["vector"].tensor_copy(difs[idx], pt[:], _inc=(sem_v, 1))
        v_dif = R.bump("v")
        psum_set("pt", sem_v, v_dif)
        E["scalar"].wait(sem_v, v_dif)
        buf_use("scalar", "xnt_sq")
        E["scalar"].activation(xnt[:], difs[idx], AF.Square,
                               accum_out=statL[:, 0 + idx:1 + idx], _inc=(sem_a, 1))
        a_m2 = R.bump("a")
        buf_set("xnt_sq", sem_a, a_m2)
        return a_m2

    def stats_batch(nw, o=0, a_need=None):
        c0, c1_, c2_, c3, c4, c5, c6, c7 = (
            statL[:, o:o + nw], statL[:, 4 + o:4 + o + nw],
            statL[:, 8 + o:8 + o + nw], statL[:, 12 + o:12 + o + nw],
            statL[:, 16 + o:16 + o + nw], statL[:, 20 + o:20 + o + nw],
            statL[:, 24 + o:24 + o + nw], statL[:, 28 + o:28 + o + nw])
        E["vector"].wait(sem_a, a_need if a_need is not None else R.val("a"))
        E["vector"].tensor_scalar_max(c1_, c0, 1e-24, _inc=(sem_v, 1))
        v1 = R.bump("v")
        E["scalar"].wait(sem_v, v1)
        E["scalar"].activation(c2_, c1_, AF.Ln, _inc=(sem_a, 1))
        aln = R.bump("a")
        E["scalar"].wait(sem_a, aln)
        E["scalar"].activation(c3, c2_, AF.Exp, scale=0.5)
        E["scalar"].activation(c4, c2_, AF.Exp, scale=-0.5, _inc=(sem_a, 1))
        amag = R.bump("a")
        E["scalar"].wait(sem_a, amag)
        E["scalar"].activation(c5, c3, AF.Exp, scale=-4.0, _inc=(sem_a, 1))
        au = R.bump("a")
        E["vector"].wait(sem_a, au)
        E["vector"].tensor_single_scalar(c6, c0, 1e-12, ALU.is_gt)
        E["vector"].tensor_scalar(c7, c5, 1.0, None, ALU.add, _inc=(sem_v, 1))
        v2 = R.bump("v")
        E["vector"].wait(sem_v, v2)
        E["vector"].tensor_mul(c4, c4, c6)
        E["vector"].reciprocal(c7, c7, _inc=(sem_v, 1))
        v3 = R.bump("v")
        E["vector"].wait(sem_v, v3)
        E["vector"].tensor_scalar(c5, c5, -1.0, 1.0, ALU.mult, ALU.add)
        E["vector"].tensor_scalar(c6, c4, 2.0, None, ALU.mult, _inc=(sem_v, 1))
        v4 = R.bump("v")
        E["vector"].wait(sem_v, v4)
        E["vector"].tensor_mul(c5, c5, c7, _inc=(sem_v, 1))
        v5 = R.bump("v")
        E["vector"].wait(sem_v, v5)
        E["vector"].scalar_tensor_tensor(c2_, c5, 0.6, c4, ALU.mult, ALU.mult)
        E["vector"].tensor_scalar(c3, c5, 0.4, None, ALU.mult)
        buf_use("vector", "mwb4")
        E["vector"].tensor_copy(mwb4[:, 1 + o:1 + o + nw], c5, _inc=(sem_v, 1))
        return R.bump("v")

    def dens_batch(nw, v_stats, first_tile_mfirst, o=0):
        E["tensor"].wait(sem_v, v_stats)
        psum_use("pd")
        if first_tile_mfirst:
            E["tensor"].matmul(pd[:, 1:2], mfirstb[:], mwb4[:, 1:2],
                               start=True, stop=True)
            E["tensor"].matmul(pd[:, 2:1 + nw], mbandb[:], mwb4[:, 2:1 + nw],
                               start=True, stop=False)
            E["tensor"].matmul(pd[:, 2:1 + nw], mprevb[:], mwb4[:, 1:nw],
                               start=False, stop=True, _inc=(sem_t, 1))
        else:
            E["tensor"].matmul(pd[:, 1 + o:1 + o + nw], mbandb[:],
                               mwb4[:, 1 + o:1 + o + nw],
                               start=True, stop=False)
            E["tensor"].matmul(pd[:, 1 + o:1 + o + nw], mprevb[:],
                               mwb4[:, o:o + nw],
                               start=False, stop=True, _inc=(sem_t, 1))
        t_den = R.bump("t")
        E["vector"].wait(sem_t, t_den)
        E["vector"].tensor_scalar_max(statL[:, 36 + o:36 + o + nw],
                                      pd[:, 1 + o:1 + o + nw], 1e-8,
                                      _inc=(sem_v, 1))
        v_dmax = R.bump("v")
        psum_set("pd", sem_v, v_dmax)
        E["vector"].wait(sem_v, v_dmax)
        E["vector"].reciprocal(statL[:, 36 + o:36 + o + nw],
                               statL[:, 36 + o:36 + o + nw],
                               _inc=(sem_v, 1))
        return R.bump("v")

    def tile_dots_mm(idx, dcm):
        psum_use("pd")
        for i in range(DB):
            kw = {} if i < DB - 1 else {"_inc": (sem_t, 1)}
            E["tensor"].matmul(pd[:], dcm[:, i, :], guid_nT[:, i, :],
                               start=(i == 0), stop=(i == DB - 1), **kw)
        return R.bump("t")

    def tile_exp(idx, t_dots, v_stats, barrier_t=False):
        E["scalar"].wait(sem_t, R.val("t") if barrier_t else t_dots)
        E["scalar"].wait(sem_v, v_stats)
        buf_use("scalar", "exp_sb")
        E["scalar"].activation(exp_sb, pd[:], AF.Exp,
                               scale=statL[:, 24 + idx:25 + idx],
                               accum_out=statL[:, 28 + idx:29 + idx], _inc=(sem_a, 1))
        a_exp = R.bump("a")
        psum_set("pd", sem_a, a_exp)
        E["vector"].wait(sem_a, a_exp)
        E["vector"].reciprocal(statL[:, 32 + idx:33 + idx],
                               statL[:, 28 + idx:29 + idx], _inc=(sem_v, 1))
        v_rs = R.bump("v")
        E["vector"].wait(sem_v, v_rs)
        E["vector"].tensor_mul(statL[:, 32 + idx:33 + idx],
                               statL[:, 32 + idx:33 + idx],
                               statL[:, 12 + idx:13 + idx], _inc=(sem_v, 1))
        v_c2r = R.bump("v")
        return a_exp, v_c2r

    def tile_gw(idx, a_exp):
        E["tensor"].wait(sem_a, a_exp)
        psum_use("pt")
        for gi in range(GB):
            kw = {} if gi < GB - 1 else {"_inc": (sem_t, 1)}
            E["tensor"].transpose(pt[:, gi * 128:(gi + 1) * 128],
                                  exp_sb[:, gi * 128:(gi + 1) * 128], identb[:], **kw)
        t_gwt = R.bump("t")
        buf_set("exp_sb", sem_t, t_gwt)
        buf_use("vector", "gwT")
        E["vector"].wait(sem_t, t_gwt)
        E["vector"].tensor_copy(gwT[:], _r(pt[:, 0:512], "p (i t) -> p i t", i=GB),
                                _inc=(sem_v, 1))
        v_gwT = R.bump("v")
        psum_set("pt", sem_v, v_gwT)
        E["tensor"].wait(sem_v, v_gwT)
        psum_use("pa")
        for gi in range(GB):
            for h2 in range(2):
                kw = {} if not (gi == GB - 1 and h2 == 1) else {"_inc": (sem_t, 1)}
                E["tensor"].matmul(pa[:, h2 * 512:(h2 + 1) * 512], gwT[:, gi, :],
                                   guidb[:, gi, h2 * 512:(h2 + 1) * 512],
                                   start=(gi == 0), stop=(gi == GB - 1), **kw)
        t_inf = R.bump("t")
        buf_set("gwT", sem_t, t_inf)
        return t_inf

    def tile_t2blend(idx, sl, t_inf, v_c2r, gc_need=None):
        bwc = bw[sl]
        if gc_need is not None:
            E["vector"].wait(sem_gc, gc_need)
        E["vector"].wait(sem_t, t_inf)
        E["vector"].wait(sem_v, v_c2r)
        buf_use("vector", "t2")
        E["vector"].tensor_scalar(t2, pa[:], statL[:, 32 + idx:33 + idx], None,
                                  ALU.mult, _inc=(sem_v, 1))
        v_t2 = R.bump("v")
        psum_set("pa", sem_v, v_t2)
        E["vector"].wait(sem_v, v_t2)
        buf_use("vector", "bw%d" % sl)
        E["vector"].scalar_tensor_tensor(bwc[:], difs[idx], statL[:, 8 + idx:9 + idx],
                                         t2, ALU.mult, ALU.add, _inc=(sem_v, 1))
        v_bw = R.bump("v")
        buf_set("difv%d" % idx, sem_v, v_bw)
        buf_set("t2", sem_v, v_bw)
        return v_bw

    def tile_numf1(idx, sl, dcm, x1src, xnTsrc, mmat, use_prev, out_rows,
                   v_rden, v_bw):
        bwc, bwh = bw[sl], bw[1 - sl]
        E["tensor"].wait(sem_v, v_bw)
        psum_use("pb")
        for h2 in range(2):
            kw = {} if (use_prev or h2 == 0) else {"_inc": (sem_t, 1)}
            E["tensor"].matmul(pb[:, h2 * 512:(h2 + 1) * 512], mmat[:],
                               bwc[:, h2 * 512:(h2 + 1) * 512],
                               start=True, stop=(not use_prev), **kw)
        if use_prev:
            for h2 in range(2):
                kw = {} if h2 == 0 else {"_inc": (sem_t, 1)}
                E["tensor"].matmul(pb[:, h2 * 512:(h2 + 1) * 512], mprevb[:],
                                   bwh[:, h2 * 512:(h2 + 1) * 512],
                                   start=False, stop=True, **kw)
        t_num = R.bump("t")
        buf_set("bw%d" % (1 - sl), sem_t, t_num)
        psum_use("pc")
        for i in range(DB):
            for h2 in range(2):
                kw = {} if not (i == DB - 1 and h2 == 1) else {"_inc": (sem_t, 1)}
                E["tensor"].matmul(pc[:, h2 * 512:(h2 + 1) * 512], xnTsrc[:, i, :],
                                   wl_tern[:, i, h2 * 512:(h2 + 1) * 512],
                                   start=(i == 0), stop=(i == DB - 1), **kw)
        t_f1 = R.bump("t")
        E["scalar"].wait(sem_t, t_f1)
        buf_use("scalar", "g1t")
        E["scalar"].activation(g1t, pc[:], AF.Silu, scale=scaleB[:, 0:1],
                               _inc=(sem_a, 1))
        a_g1 = R.bump("a")
        psum_set("pc", sem_a, a_g1)
        E["vector"].wait(sem_v, v_rden)
        E["vector"].wait(sem_t, t_num)
        buf_use("vector", "tr1")
        E["vector"].tensor_scalar(tr1, pb[:], statL[:, 36 + idx:37 + idx], None,
                                  ALU.mult, _inc=(sem_v, 1))
        v_tr1 = R.bump("v")
        psum_set("pb", sem_v, v_tr1)
        E["gpsimd"].wait(sem_v, v_tr1)
        E["gpsimd"].wait(sem_a, a_g1)
        buf_use("gpsimd", "tg")
        E["gpsimd"].tensor_mul(tg, tr1, g1t, _inc=(sem_gc, 1))
        g_tg = R.bump("gc")
        buf_set("tr1", sem_gc, g_tg)
        buf_set("g1t", sem_gc, g_tg)
        E["vector"].wait(sem_gc, g_tg)
        E["vector"].tensor_add(tg, tg, x1src, _inc=(sem_v, 1))
        v_x2 = R.bump("v")
        E["sync"].wait(sem_v, v_x2)
        E["sync"].dma_start(out=_r(out_d[out_rows:out_rows + 128, :], "(o p) e -> p (o e)", p=128),
                            in_=tg, _inc=(sem_st, 16))
        st_c = R.bump("st", 16)
        buf_set("tg", sem_st, st_c)

    def tile_dots(idx, dcm, v_stats):
        return tile_exp(idx, tile_dots_mm(idx, dcm), v_stats)

    def tile_pipeline(tiles, v_stats, v_rden, idx0=0):
        n = len(tiles)
        st = {0: tile_dots(idx0, tiles[0][0], v_stats)}
        ti = {0: tile_gw(idx0, st[0][0])}
        for i in range(n):
            idx = idx0 + i
            sl = pt_seq[0] % 2
            v_bw = tile_t2blend(idx, sl, ti[i], st[i][1])
            if i + 1 < n:
                st[i + 1] = tile_dots(idx + 1, tiles[i + 1][0], v_stats)
            dcm, x1src, xnTsrc, mmat, use_prev, out_rows = tiles[i]
            tile_numf1(idx, sl, dcm, x1src, xnTsrc, mmat, use_prev, out_rows,
                       v_rden, v_bw)
            if i + 1 < n:
                ti[i + 1] = tile_gw(idx + 1, st[i + 1][0])
            pt_seq[0] += 1

    sq_fc = {}
    rstd0 = None
    for c in range(NCH):
        xcur = xa if c % 2 == 0 else xb
        xnext = xb if c % 2 == 0 else xa
        if c + 1 < NCH:
            E["sync"].wait(sem_st, R.val("st"))
            if c - 1 in carry_v:
                E["sync"].wait(sem_v, carry_v[c - 1])
            par = (c + 1) % 2
            E["sync"].dma_start(
                out=xnext[:],
                in_=_r(x_in[(c + 1) * T:(c + 2) * T, :], "(j p) e -> p j e", p=128),
                _inc=(sem_x[par], 16))
            x_ld[c + 1] = (par, R.bump("x%d" % par if par == 0 else "x1s", 16))

        if c == 0:
            E["scalar"].wait(sem_x[x_ld[c][0]], x_ld[c][1])
            E["scalar"].wait(sem_v, R.val("v"))
            E["vector"].wait(sem_x[x_ld[c][0]], x_ld[c][1])
            for j in range(NT):
                ln_sq_tile(xcur, j, 0)
            rstd0 = ln_finish(0)
        xn0_ap = ln_apply(xcur, 0, xn0T, rstd0)
        E["tensor"].wait(sem_v, xn0_ap)

        def kvr(w_int, mode, dst):
            cv = None if c > 0 else w_int
            srcap = w_int if c > 0 else cvt_src[w_int.tensor.name]
            sl_, ldt, wv = ring_load(srcap, DB, D, cvt=cv)
            E["tensor"].wait(SEMMAP[ldt[0]], ldt[1])
            evac = {}
            for eb in range(DB):
                key = "pa" if eb % 2 == 0 else "pb"
                reg = (pa if eb % 2 == 0 else pb)[:, 0:T]
                psum_use(key)
                for i in range(DB):
                    kw = {} if i < DB - 1 else {"_inc": (sem_t, 1)}
                    E["tensor"].matmul(reg, wv[:, i, eb * 128:(eb + 1) * 128],
                                       xn0T[:, i, :], start=(i == 0),
                                       stop=(i == DB - 1), **kw)
                tmm = R.bump("t")
                if mode == "s_copy":
                    E["scalar"].wait(sem_t, tmm)
                    E["scalar"].copy(dst[:, eb, :], reg, _inc=(sem_a, 1))
                    evac[eb] = R.bump("a")
                    psum_set(key, sem_a, evac[eb])
                elif mode == "s_tanh":
                    E["scalar"].wait(sem_t, tmm)
                    E["scalar"].activation(dst[:, eb, :], reg, AF.Tanh, scale=0.5,
                                           _inc=(sem_a, 1))
                    evac[eb] = R.bump("a")
                    psum_set(key, sem_a, evac[eb])
                else:
                    E["vector"].wait(sem_t, tmm)
                    E["vector"].tensor_copy(dst[:, eb, :], reg, _inc=(sem_v, 1))
                    evac[eb] = R.bump("v")
                    psum_set(key, sem_v, evac[eb])
            ring_release(sl_)
            return evac[DB - 1]

        k_done = kvr(wk_i, "s_copy", kbf)
        v_done = kvr(wv_i, "v_copy", vbf)
        r_done = kvr(wr_i, "s_tanh", trs)
        E["vector"].wait(sem_v, v_done)

        E["vector"].wait(sem_st, R.val("st"))
        E["vector"].wait(sem_a, k_done)
        E["vector"].tensor_mul(kbf[:], kbf[:], vbf[:], _inc=(sem_v, 1))
        v_kv = R.bump("v")
        E["vector"].wait(sem_v, v_kv)
        for i in range(DB):
            kw = {} if i < DB - 1 else {"_inc": (sem_v, 1)}
            E["vector"].tensor_tensor_scan(
                zs[:, i, 0:T], dec[:, i:i + 1].broadcast_to([128, T]),
                kbf[:, i, :], zcarry[:, i, 0:1], ALU.mult, ALU.add, **kw)
        v_scan = R.bump("v")
        E["vector"].wait(sem_v, v_scan)
        for i in range(DB):
            E["vector"].scalar_tensor_tensor(
                vbf[:, i, 0:1], kbf[:, i, 0:1], eu[:, i:i + 1], zcarry[:, i, 0:1],
                ALU.mult, ALU.add)
            kw = {} if i < DB - 1 else {"_inc": (sem_v, 1)}
            E["vector"].scalar_tensor_tensor(
                vbf[:, i, 1:T], kbf[:, i, 1:T], eu[:, i:i + 1], zs[:, i, 0:T - 1],
                ALU.mult, ALU.add, **kw)
        v_wkv = R.bump("v")
        E["vector"].wait(sem_v, v_wkv)
        E["vector"].tensor_copy(zcarry[:], zs[:, :, T - 1:T])
        E["vector"].wait(sem_a, r_done)
        E["vector"].scalar_tensor_tensor(trs[:], trs[:], 1.0, vbf[:], ALU.add,
                                         ALU.mult, _inc=(sem_v, 1))
        s0_done = R.bump("v")

        if 0 < c < NCH - 1:
            E["scalar"].wait(sem_x[x_ld[c + 1][0]], x_ld[c + 1][1])
            E["vector"].wait(sem_x[x_ld[c + 1][0]], x_ld[c + 1][1])
            buf_use("scalar", "xnt_sq")
            sqh = None
            for j in range(NT):
                if j > 0:
                    E["scalar"].wait(sem_a, sqh)
                E["scalar"].activation(xnt[:], xnext[:, j, :], AF.Square,
                                       accum_out=stats[:, 8 * j + 1:8 * j + 2],
                                       _inc=(sem_a, 1))
                sqh = R.bump("a")
                E["vector"].reduce_sum(stats[:, 8 * j:8 * j + 1], xnext[:, j, :],
                                       axis=AX, _inc=(sem_v, 1))
                R.bump("v")
            buf_set("xnt_sq", sem_a, sqh)
        E["vector"].wait(sem_st, R.val("st"))
        for q in range(2):
            colr = slice(q * HID // 2, (q + 1) * HID // 2)
            sl_u, ldt, wuv = ring_load(
                wu_i[:, :, colr] if c > 0 else cvt_src[wu_i.tensor.name][:, :, colr],
                DB, HID // 2, cvt=None if c > 0 else wu_i[:, :, colr])
            E["tensor"].wait(SEMMAP[ldt[0]], ldt[1])
            upev = {}
            for hb in range(HQ):
                key = "pa" if hb % 2 == 0 else "pb"
                reg = (pa if hb % 2 == 0 else pb)[:, 0:T]
                psum_use(key)
                for i in range(DB):
                    kw = {} if i < DB - 1 else {"_inc": (sem_t, 1)}
                    E["tensor"].matmul(reg, wuv[:, i, hb * 128:(hb + 1) * 128],
                                       xn0T[:, i, :], start=(i == 0),
                                       stop=(i == DB - 1), **kw)
                tmm = R.bump("t")
                if hb == 0 and q >= 1:
                    E["vector"].wait(sem_v, buf_guard.get("sgt_q", (sem_v, 0))[1])
                E["vector"].wait(sem_t, tmm)
                E["vector"].tensor_copy(up_sb[:, hb, :], reg, _inc=(sem_v, 1))
                upev[hb] = R.bump("v")
                psum_set(key, sem_v, upev[hb])
            ring_release(sl_u)
            sl_g, ldt, wgv = ring_load(
                wg_i[:, :, colr] if c > 0 else cvt_src[wg_i.tensor.name][:, :, colr],
                DB, HID // 2, cvt=None if c > 0 else wg_i[:, :, colr])
            E["tensor"].wait(SEMMAP[ldt[0]], ldt[1])
            sgc = {}
            hdone = {}
            for hb in range(HQ):
                key = "pa" if hb % 2 == 0 else "pb"
                reg = (pa if hb % 2 == 0 else pb)[:, 0:T]
                psum_use(key)
                for i in range(DB):
                    kw = {} if i < DB - 1 else {"_inc": (sem_t, 1)}
                    E["tensor"].matmul(reg, wgv[:, i, hb * 128:(hb + 1) * 128],
                                       xn0T[:, i, :], start=(i == 0),
                                       stop=(i == DB - 1), **kw)
                tmm = R.bump("t")
                if hb >= 1:
                    E["scalar"].wait(sem_v, hdone[hb - 1])
                elif q >= 1:
                    E["scalar"].wait(sem_v, buf_guard.get("sgt_q", (sem_v, 0))[1])
                E["scalar"].wait(sem_t, tmm)
                E["scalar"].activation(sgt, reg, AF.Silu, scale=1.0 / 127.0,
                                       _inc=(sem_a, 1))
                sgc[hb] = R.bump("a")
                psum_set(key, sem_a, sgc[hb])
                E["vector"].wait(sem_a, sgc[hb])
                E["vector"].tensor_mul(h_sb[:, q * HQ + hb, :], sgt, up_sb[:, hb, :],
                                       _inc=(sem_v, 1))
                hdone[hb] = R.bump("v")
            buf_set("sgt_q", sem_v, hdone[HQ - 1])
            ring_release(sl_g)
        h_all = R.val("v")

        sl_o, ldt, wov = ring_load(wo_i if c > 0 else cvt_src[wo_i.tensor.name],
                                   DB, D, cvt=None if c > 0 else wo_i)
        E["tensor"].wait(SEMMAP[ldt[0]], ldt[1])
        E["tensor"].wait(sem_v, s0_done)
        E["tensor"].wait(sem_v, h_all)
        s_evac = {}
        for j in range(NT):
            key = "pa" if j % 2 == 0 else "pb"
            reg = (pa if j % 2 == 0 else pb)[:, 0:1024]
            psum_use(key)
            for i in range(DB):
                for h2 in range(2):
                    kw = {} if not (i == DB - 1 and h2 == 1) else {"_inc": (sem_t, 1)}
                    E["tensor"].matmul(reg[:, h2 * 512:(h2 + 1) * 512],
                                       trs[:, i, j * 128:(j + 1) * 128],
                                       wov[:, i, h2 * 512:(h2 + 1) * 512],
                                       start=(i == 0), stop=(i == DB - 1), **kw)
            tmm = R.bump("t")
            E["vector"].wait(sem_t, tmm)
            E["vector"].tensor_copy(s_bf[:, j, :], reg, _inc=(sem_v, 1))
            s_evac[j] = R.bump("v")
            psum_set(key, sem_v, s_evac[j])
        ring_release(sl_o)

        sl_d0, ldt0, wdv0 = ring_load(
            wd_i[:, 0:HQ, :] if c > 0 else cvt_src[wd_i.tensor.name][:, 0:HQ, :],
            HQ, D, cvt=None if c > 0 else wd_i[:, 0:HQ, :])
        sl_d1, ldt1, wdv1 = ring_load(
            wd_i[:, HQ:HB, :] if c > 0 else cvt_src[wd_i.tensor.name][:, HQ:HB, :],
            HQ, D, cvt=None if c > 0 else wd_i[:, HQ:HB, :])
        E["tensor"].wait(SEMMAP[ldt0[0]], ldt0[1])
        E["tensor"].wait(SEMMAP[ldt1[0]], ldt1[1])
        E["tensor"].wait(sem_v, h_all)
        pregs = [("pa", pa), ("pb", pb), ("pc", pc)]
        for j in range(NT):
            key, reg = pregs[j % 3]
            psum_use(key)
            for hb in range(HB):
                wdv = wdv0 if hb < HQ else wdv1
                hbl = hb % HQ
                for h2 in range(2):
                    last = (hb == HB - 1 and h2 == 1)
                    kw = {} if not last else {"_inc": (sem_t, 1)}
                    E["tensor"].matmul(reg[:, h2 * 512:(h2 + 1) * 512],
                                       h_sb[:, hb, j * 128:(j + 1) * 128],
                                       wdv[:, hbl, h2 * 512:(h2 + 1) * 512],
                                       start=(hb == 0), stop=(hb == HB - 1), **kw)
            t_down = R.bump("t")
            g0t_j, t1t_j = g0ts[j % 2], t1ts[j % 2]
            E["scalar"].wait(sem_t, t_down)
            buf_use("scalar", "g0t%d" % (j % 2))
            E["scalar"].activation(g0t_j, reg[:], AF.Silu, scale=1.0 / 16129.0,
                                   _inc=(sem_a, 1))
            ag0 = R.bump("a")
            psum_set(key, sem_a, ag0)
            E["gpsimd"].wait(sem_a, ag0)
            E["gpsimd"].wait(sem_v, s_evac[j])
            buf_use("gpsimd", "t1t%d" % (j % 2))
            E["gpsimd"].tensor_mul(t1t_j, s_bf[:, j, :], g0t_j, _inc=(sem_gc, 1))
            gmul = R.bump("gc")
            buf_set("g0t%d" % (j % 2), sem_gc, gmul)
            E["vector"].wait(sem_gc, gmul)
            E["vector"].scalar_tensor_tensor(xcur[:, j, :], t1t_j, 0.5,
                                             xcur[:, j, :], ALU.mult, ALU.add,
                                             _inc=(sem_v, 1))
            vx1 = R.bump("v")
            buf_set("t1t%d" % (j % 2), sem_v, vx1)
            E["scalar"].wait(sem_v, vx1)
            E["vector"].wait(sem_v, vx1)
            ln_sq_tile(xcur, j, 32)
            if c > 0:
                if j == 0:
                    a_pre0 = tile_pre(deprev[:], 0, v_need=carry_v[c - 1])
                elif j == 1:
                    v_st0 = stats_batch(1, 0, a_need=a_pre0)
        ring_release(sl_d0)
        ring_release(sl_d1)

        if c > 0:
            v_rd0 = dens_batch(1, v_st0, False, 0)
            t_d0 = tile_dots_mm(0, deprev[:])
            st0 = tile_exp(0, t_d0, v_st0, barrier_t=True)
            ti0 = tile_gw(0, st0[0])
            sl0 = pt_seq[0] % 2
            v_bw0 = tile_t2blend(0, sl0, ti0, st0[1], gc_need=R.val("gc"))
            tile_numf1(0, sl0, deprev[:], x1prev[:], xn1Tprev[:], mbandb, True,
                       (c - 1) * T + (NT - 1) * 128, v_rd0, v_bw0)
            pt_seq[0] += 1

        if c == 0:
            emit_l1_prep()

        rstd1 = ln_finish(32)
        if 0 < c < NCH - 1:
            rstd0 = ln_finish(0)
        xn1T_done = ln_apply(xcur, 32, xn1T, rstd1)
        E["vector"].wait(sem_v, xn1T_done)
        if c > 0:
            E["vector"].tensor_sub(deprev[:, :, 127:128], xn1T[:, :, 0:1],
                                   xn1last[:, :, 0:1])
        buf_use("vector", "de")
        E["vector"].tensor_sub(de[:, :, 0:T - 1], xn1T[:, :, 1:T],
                               xn1T[:, :, 0:T - 1])
        E["vector"].memset(de[:, :, T - 1:T], 0.0, _inc=(sem_v, 1))
        R.bump("v")

        idx0 = 1 if c > 0 else 0
        tiles = []
        for jt in range(NT - 1):
            mm = mfirstb if (c == 0 and jt == 0) else mbandb
            useprev = not (c == 0 and jt == 0)
            tiles.append((de[:, :, jt * 128:(jt + 1) * 128], xcur[:, jt, :],
                          xn1T[:, :, jt * 128:(jt + 1) * 128], mm, useprev,
                          c * T + jt * 128))
        for i, tl in enumerate(tiles):
            tile_pre(tl[0], idx0 + i)
        v_stats = stats_batch(len(tiles), idx0)
        v_rden = dens_batch(len(tiles), v_stats, c == 0, idx0)
        tile_pipeline(tiles, v_stats, v_rden, idx0)

        if c == 0:
            xnx = xb if c % 2 == 0 else xa
            E["scalar"].wait(sem_x[x_ld[c + 1][0]], x_ld[c + 1][1])
            E["vector"].wait(sem_x[x_ld[c + 1][0]], x_ld[c + 1][1])
            for j in range(NT):
                ln_sq_tile(xnx, j, 0)
            rstd0 = ln_finish(0)
        E["vector"].wait(sem_v, R.val("v"))
        E["vector"].tensor_copy(mwb4[:, 0:1], mwb4[:, idx0 + 3:idx0 + 4],
                                _inc=(sem_v, 1))
        R.bump("v")

        tlast = R.val("t")
        E["vector"].wait(sem_t, tlast)
        E["vector"].tensor_copy(deprev[:], de[:, :, (NT - 1) * 128:T])
        E["vector"].tensor_copy(xn1last[:], xn1T[:, :, T - 1:T])
        E["vector"].tensor_copy(xn1Tprev[:], xn1T[:, :, (NT - 1) * 128:T])
        E["vector"].tensor_copy(x1prev[:], xcur[:, NT - 1, :], _inc=(sem_v, 1))
        v_carry = R.bump("v")
        carry_v[c] = v_carry
        buf_set("de", sem_v, v_carry)

    a_pre0 = tile_pre(deprev[:], 0, v_need=carry_v[NCH - 1])
    v_st0 = stats_batch(1, 0, a_need=a_pre0)
    v_rd0 = dens_batch(1, v_st0, False, 0)
    st0 = tile_dots(0, deprev[:], v_st0)
    ti0 = tile_gw(0, st0[0])
    sl0 = pt_seq[0] % 2
    v_bw0 = tile_t2blend(0, sl0, ti0, st0[1])
    tile_numf1(0, sl0, deprev[:], x1prev[:], xn1Tprev[:], mbandb, True,
               (NCH - 1) * T + (NT - 1) * 128, v_rd0, v_bw0)
    pt_seq[0] += 1
    E["sync"].wait(sem_st, R.val("st"))

    with es:
        block = es.enter_context(nc.Block())
        block.sync(lambda e: [fn(e) for fn in R.prog["sync"]])
        block.gpsimd(lambda e: [fn(e) for fn in R.prog["gpsimd"]])
        block.tensor(lambda e: [fn(e) for fn in R.prog["tensor"]])
        block.vector(lambda e: [fn(e) for fn in R.prog["vector"]])
        block.scalar(lambda e: [fn(e) for fn in R.prog["scalar"]])
    return nc


def host_prep(inputs):
    f32 = np.float32
    Mband, Mprev, Mfirst = make_M_matrices()
    M64b, M64p, M64f = make_M_matrices_f64()
    Mlo = np.concatenate([(M64b - Mband.astype(np.float64)).astype(f32),
                          (M64p - Mprev.astype(np.float64)).astype(f32),
                          (M64f - Mfirst.astype(np.float64)).astype(f32)], axis=1)
    return {
        "wkT": np.ascontiguousarray(np.asarray(inputs["L0_attn_Wk"], f32).T),
        "wvT": np.ascontiguousarray(np.asarray(inputs["L0_attn_Wv"], f32).T),
        "wrT": np.ascontiguousarray(np.asarray(inputs["L0_attn_Wr"], f32).T),
        "woT": np.ascontiguousarray(np.asarray(inputs["L0_attn_Wo"], f32).T),
        "wgT": np.ascontiguousarray(np.asarray(inputs["L0_ffn_w_gate"], np.int32).T),
        "wuT": np.ascontiguousarray(np.asarray(inputs["L0_ffn_w_up"], np.int32).T),
        "wdT": np.ascontiguousarray(np.asarray(inputs["L0_ffn_w_down"], np.int32).T),
        "td_c": np.ascontiguousarray(np.asarray(inputs["L0_attn_time_decay"], f32).reshape(DB, 128).T),
        "tf_c": np.ascontiguousarray(np.asarray(inputs["L0_attn_time_first"], f32).reshape(DB, 128).T),
        "guid": np.ascontiguousarray(np.asarray(inputs["L1_attn_guidance_trajectories"], f32)),
        "wlT": np.ascontiguousarray(np.asarray(inputs["L1_ffn_weight"], f32).T),
        "ident": np.eye(128, dtype=ml_dtypes.bfloat16),
        "cvec_in": np.broadcast_to(np.array([1e-5, np.log(2.0), 0.0, 0.0], f32), (128, 4)).copy(),
        "mband": Mband, "mprev": Mprev, "mfirst": Mfirst,
        "mlo": Mlo,
        "onesc": np.ones((128, 1), f32), "onesr": np.ones((1, 128), f32),
    }


_NC_CACHE = {}


def kernel(**inputs):
    x = np.asarray(inputs["L0_attn_x"], np.float32)
    B = x.shape[0]
    if "nc" not in _NC_CACHE:
        _NC_CACHE["nc"] = build_fused()
    nc = _NC_CACHE["nc"]
    shared = host_prep(inputs)
    in_maps = []
    for b in range(B):
        m = dict(shared)
        m["x"] = np.ascontiguousarray(x[b])
        in_maps.append(m)
    res = run_bass_kernel_spmd(nc, in_maps, core_ids=list(range(B)))
    return np.stack([r["out"] for r in res.results], axis=0)









# version 23
# speedup vs baseline: 1.0519x; 1.0200x over previous
import contextlib
import numpy as np
import ml_dtypes
import concourse.bass as bass
import concourse.mybir as mybir
from concourse.bass_utils import run_bass_kernel_spmd

F32 = mybir.dt.float32
BF16 = mybir.dt.bfloat16
I32 = mybir.dt.int32
AF = mybir.ActivationFunctionType
ALU = mybir.AluOpType
AX = mybir.AxisListType.X

S, D, HID, G = 4096, 1024, 3072, 512
DB = D // 128
HB = HID // 128
GB = G // 128

ENGS = ("sync", "gpsimd", "tensor", "vector", "scalar")


class Rec:
    def __init__(self):
        self.prog = {k: [] for k in ENGS}
        self.cnt = {}

    def eng(self, name):
        return _EngProxy(self, name)

    def bump(self, sem_name, n=1):
        self.cnt[sem_name] = self.cnt.get(sem_name, 0) + n
        return self.cnt[sem_name]

    def val(self, sem_name):
        return self.cnt.get(sem_name, 0)


class _EngProxy:
    def __init__(self, rec, name):
        self._rec = rec
        self._name = name

    def wait(self, sem, val):
        if val > 0:
            self._rec.prog[self._name].append(lambda e: e.wait_ge(sem, val))

    def __getattr__(self, attr):
        rec, name = self._rec, self._name

        def call(*args, _inc=None, **kwargs):
            def fn(eng, attr=attr, args=args, kwargs=kwargs, _inc=_inc):
                ins = getattr(eng, attr)(*args, **kwargs)
                if _inc is not None:
                    ins.then_inc(*_inc)
            rec.prog[name].append(fn)
        return call


def _r(t, pat, **kw):
    return t.rearrange(pat, **kw)


def make_M_matrices():
    W = 8

    def pw(s, k):
        idx = s - W + k
        if idx < 0 or s < 1:
            return 0.0
        w_cnt = min(s, W)
        j = k - (W - w_cnt)
        if w_cnt > 1:
            return 0.1 + 0.9 * j / (w_cnt - 1)
        return 0.1

    Mband = np.zeros((128, 128), np.float64)
    Mprev = np.zeros((128, 128), np.float64)
    Mfirst = np.zeros((128, 128), np.float64)
    for sl in range(128):
        for k in range(W):
            jl = sl - W + k
            if jl >= 0:
                Mband[jl, sl] = pw(128 + sl, k)
                Mfirst[jl, sl] = pw(sl, k)
            else:
                Mprev[128 + jl, sl] = pw(128 + sl, k)
    return (Mband.astype(np.float32), Mprev.astype(np.float32),
            Mfirst.astype(np.float32))


def make_M_matrices_f64():
    W = 8

    def pw(s, k):
        idx = s - W + k
        if idx < 0 or s < 1:
            return 0.0
        w_cnt = min(s, W)
        j = k - (W - w_cnt)
        if w_cnt > 1:
            return 0.1 + 0.9 * j / (w_cnt - 1)
        return 0.1

    Mband = np.zeros((128, 128), np.float64)
    Mprev = np.zeros((128, 128), np.float64)
    Mfirst = np.zeros((128, 128), np.float64)
    for sl in range(128):
        for k in range(W):
            jl = sl - W + k
            if jl >= 0:
                Mband[jl, sl] = pw(128 + sl, k)
                Mfirst[jl, sl] = pw(sl, k)
            else:
                Mprev[128 + jl, sl] = pw(128 + sl, k)
    return Mband, Mprev, Mfirst


def build_fused(T=512):
    NT = T // 128
    NCH = S // T
    HQ = 12
    nc = bass.Bass("TRN2", target_bir_lowering=False, debug=False)
    x_in = nc.dram_tensor("x", [S, D], F32, kind="ExternalInput").ap()
    wkT = nc.dram_tensor("wkT", [D, D], F32, kind="ExternalInput").ap()
    wvT = nc.dram_tensor("wvT", [D, D], F32, kind="ExternalInput").ap()
    wrT = nc.dram_tensor("wrT", [D, D], F32, kind="ExternalInput").ap()
    woT = nc.dram_tensor("woT", [D, D], F32, kind="ExternalInput").ap()
    wgT = nc.dram_tensor("wgT", [D, HID], I32, kind="ExternalInput").ap()
    wuT = nc.dram_tensor("wuT", [D, HID], I32, kind="ExternalInput").ap()
    wdT = nc.dram_tensor("wdT", [HID, D], I32, kind="ExternalInput").ap()
    td_c = nc.dram_tensor("td_c", [128, DB], F32, kind="ExternalInput").ap()
    tf_c = nc.dram_tensor("tf_c", [128, DB], F32, kind="ExternalInput").ap()
    guid_d = nc.dram_tensor("guid", [G, D], F32, kind="ExternalInput").ap()
    wlT_d = nc.dram_tensor("wlT", [D, D], F32, kind="ExternalInput").ap()
    ident_d = nc.dram_tensor("ident", [128, 128], BF16, kind="ExternalInput").ap()
    cvec_d = nc.dram_tensor("cvec_in", [128, 4], F32, kind="ExternalInput").ap()
    mband_d = nc.dram_tensor("mband", [128, 128], F32, kind="ExternalInput").ap()
    mprev_d = nc.dram_tensor("mprev", [128, 128], F32, kind="ExternalInput").ap()
    mfirst_d = nc.dram_tensor("mfirst", [128, 128], F32, kind="ExternalInput").ap()
    mlo_d = nc.dram_tensor("mlo", [128, 384], F32, kind="ExternalInput").ap()
    onesc_d = nc.dram_tensor("onesc", [128, 1], F32, kind="ExternalInput").ap()
    onesr_d = nc.dram_tensor("onesr", [1, 128], F32, kind="ExternalInput").ap()
    out_d = nc.dram_tensor("out", [S, D], F32, kind="ExternalOutput").ap()
    wk_i = nc.dram_tensor("wk_i", [128, DB, D], BF16, kind="Internal").ap()
    wv_i = nc.dram_tensor("wv_i", [128, DB, D], BF16, kind="Internal").ap()
    wr_i = nc.dram_tensor("wr_i", [128, DB, D], BF16, kind="Internal").ap()
    wo_i = nc.dram_tensor("wo_i", [128, DB, D], BF16, kind="Internal").ap()
    wg_i = nc.dram_tensor("wg_i", [128, DB, HID], BF16, kind="Internal").ap()
    wu_i = nc.dram_tensor("wu_i", [128, DB, HID], BF16, kind="Internal").ap()
    wd_i = nc.dram_tensor("wd_i", [128, HB, D], BF16, kind="Internal").ap()

    es = contextlib.ExitStack()
    sbt = lambda name, shape, dt: es.enter_context(nc.sbuf_tensor(name, shape, dt))
    ring = [sbt("ring0", [128, 12288], BF16), sbt("ring1", [128, 12288], BF16)]
    argA1 = sbt("argA1", [128, 4096], F32)
    argA2 = sbt("argA2", [128, 12288], BF16)
    argB = sbt("argB", [128, 6144], BF16)
    argC = sbt("argC", [128, 4096], BF16)
    xa = sbt("xa", [128, NT, D], F32)
    xb = sbt("xb", [128, NT, D], F32)
    xn0T = sbt("xn0T", [128, DB, T], BF16)
    trs = sbt("trs", [128, DB, T], BF16)
    zcarry = sbt("zcarry", [128, DB, 1], F32)
    zs = _r(argA1[:, 0:DB * T], "p (i t) -> p i t", i=DB)
    stg2 = _r(argA1[:, 0:4 * D], "p (g e) -> p g e", g=4)
    tr1 = argA1[:, 0:D]
    tg = argA1[:, D:2 * D]
    x2t = tg
    mstg = argA1[:, 2 * D:2 * D + 384]
    mlo_s = argA1[:, 2 * D + 384:2 * D + 768]
    h_sb = _r(argA2[:, 0:HB * T], "p (i t) -> p i t", i=HB)
    gnrm = argA2[:, 0:D]
    de = _r(argA2[:, 0:DB * T], "p (i t) -> p i t", i=DB)
    xn1T = _r(argA2[:, DB * T:2 * DB * T], "p (i t) -> p i t", i=DB)
    difs = [argA2[:, 2 * DB * T:2 * DB * T + D], argB[:, 5120:5120 + D],
            argB[:, 4096:4096 + D], argB[:, 0:D]]
    g1t = argA2[:, 2 * DB * T + D:2 * DB * T + 2 * D]
    gw = argA2[:, 2 * DB * T + 2 * D:2 * DB * T + 2 * D + G]
    gwTv = argA2[:, 2 * DB * T + 2 * D + G:2 * DB * T + 2 * D + 2 * G]
    exp_sb = argA2[:, 2 * DB * T + 2 * D + 2 * G:2 * DB * T + 2 * D + 3 * G]
    kbf = _r(argB[:, 0:DB * T], "p (i t) -> p i t", i=DB)
    s_bf = _r(argB[:, 0:NT * D], "p (j e) -> p j e", j=NT)
    up_sb = _r(argB[:, 0:HQ * T], "p (i t) -> p i t", i=HQ)
    sq4 = _r(argB[:, 0:NT * D], "p (j e) -> p j e", j=NT)
    vbf = _r(argC[:, 0:DB * T], "p (i t) -> p i t", i=DB)
    t2 = argC[:, 2560:2560 + D]
    sgt = argC[:, 0:T]
    g0ts = [argC[:, T:T + D], argC[:, 2560:2560 + D]]
    t1ts = [argC[:, T + D:T + 2 * D], argB[:, 4096:4096 + D]]
    xnt = sbt("xnt", [128, D], BF16)
    stats = sbt("stats", [128, 64], F32)
    statL = sbt("statL", [128, 40], F32)
    mwb4 = sbt("mwb4", [128, 8], BF16)
    tdc = sbt("tdc", [128, DB], F32)
    tfc = sbt("tfc", [128, DB], F32)
    dec = sbt("dec", [128, DB], F32)
    eu = sbt("eu", [128, DB], F32)
    deprev = sbt("deprev", [128, DB, 128], BF16)
    xn1Tprev = sbt("xn1Tprev", [128, DB, 128], BF16)
    xn1last = sbt("xn1last", [128, DB, 1], BF16)
    x1prev = sbt("x1prev", [128, D], F32)
    gwT = _r(gwTv, "p (i t) -> p i t", i=GB)
    bw = [sbt("bw0", [128, D], BF16), sbt("bw1", [128, D], BF16)]
    mwb = sbt("mwb", [128, 2], BF16)
    guid_nT = sbt("guid_nT", [128, DB, G], BF16)
    guidb = sbt("guidb", [128, GB, D], BF16)
    wl_tern = sbt("wl_tern", [128, DB, D], BF16)
    gn2 = sbt("gn2", [128, GB], F32)
    invg = sbt("invg", [128, GB], F32)
    aw = sbt("aw", [128, DB], F32)
    scl = sbt("scl", [1, 4], F32)
    scaleB = sbt("scaleB", [128, 1], F32)
    rscB = sbt("rscB", [128, 1], F32)
    identb = sbt("identb", [128, 128], BF16)
    cvec = sbt("cvec", [128, 4], F32)
    mbandb = sbt("mbandb", [128, 128], BF16)
    mprevb = sbt("mprevb", [128, 128], BF16)
    mfirstb = sbt("mfirstb", [128, 128], BF16)

    onescb = sbt("onescb", [128, 1], F32)
    onesrb = sbt("onesrb", [1, 128], F32)

    psm = lambda name, shape, dt: es.enter_context(nc.psum_tensor(name, shape, dt))
    pa = psm("pa", [128, 1024], F32)
    pb = psm("pb", [128, 1024], F32)
    pc = psm("pc", [128, 1024], F32)
    pd = psm("pd", [128, 512], F32)
    pt = psm("pt", [128, 1024], BF16)

    sem = {}
    for s_ in ("mi", "x0", "x1s", "w0", "w1", "g0", "g1", "gc", "gl", "c0", "c1", "t", "v", "a", "st"):
        sem[s_] = es.enter_context(nc.semaphore(s_))
    sem_mi, sem_t, sem_v, sem_a, sem_st, sem_gc = (
        sem["mi"], sem["t"], sem["v"], sem["a"], sem["st"], sem["gc"])
    sem_x = [sem["x0"], sem["x1s"]]
    sem_gl = sem["gl"]
    sem_c = [sem["c0"], sem["c1"]]
    sem_w = [sem["w0"], sem["w1"]]
    sem_g = [sem["g0"], sem["g1"]]

    R = Rec()
    E = {k: R.eng(k) for k in ENGS}

    psum_guard = {}

    def psum_use(key):
        g = psum_guard.get(key)
        if g is not None:
            E["tensor"].wait(g[0], g[1])

    def psum_set(key, sm, count):
        psum_guard[key] = (sm, count)

    buf_guard = {}

    def buf_use(eng, name):
        g = buf_guard.get(name)
        if g is not None:
            E[eng].wait(g[0], g[1])

    def buf_set(name, sm, count):
        buf_guard[name] = (sm, count)

    def vfence():
        E["vector"].wait(sem_v, R.val("v"))

    def afence():
        E["scalar"].wait(sem_a, R.val("a"))

    for (dst, srcp) in [(identb[:], ident_d), (cvec[:], cvec_d),
                        (mstg[:, 0:128], mband_d), (mstg[:, 128:256], mprev_d),
                        (mstg[:, 256:384], mfirst_d), (onescb[:], onesc_d),
                        (onesrb[:], onesr_d), (tdc[:], td_c), (tfc[:], tf_c),
                        (mlo_s, mlo_d)]:
        E["sync"].dma_start(out=dst, in_=srcp, _inc=(sem_mi, 16)); R.bump("mi", 16)
    mi_done0 = R.val("mi")
    E["vector"].wait(sem_mi, mi_done0)
    E["vector"].tensor_add(mstg, mstg, mlo_s, _inc=(sem_v, 1)); R.bump("v")
    E["vector"].wait(sem_v, R.val("v"))
    E["vector"].tensor_copy(mbandb[:], mstg[:, 0:128])
    E["vector"].tensor_copy(mprevb[:], mstg[:, 128:256])
    E["vector"].tensor_copy(mfirstb[:], mstg[:, 256:384], _inc=(sem_v, 1))
    v_mdone = R.bump("v")
    mi_done = R.val("mi")
    E["sync"].dma_start(out=xa[:], in_=_r(x_in[0:T, :], "(j p) e -> p j e", p=128),
                        _inc=(sem_x[0], 16))
    x_ld = {0: (0, R.bump("x0", 16))}

    E["vector"].wait(sem_mi, mi_done)
    E["vector"].memset(zcarry[:], 0.0, _inc=(sem_v, 1)); R.bump("v")
    E["scalar"].wait(sem_mi, mi_done)
    E["scalar"].activation(dec[:], tdc[:], AF.Exp, _inc=(sem_a, 1)); R.bump("a")
    afence()
    E["scalar"].activation(dec[:], dec[:], AF.Exp, scale=-1.0)
    E["scalar"].activation(eu[:], tfc[:], AF.Exp, _inc=(sem_a, 1)); R.bump("a")

    cvt_src = {}
    for (srcp, dst) in [(wkT, wk_i), (wvT, wv_i), (wrT, wr_i), (woT, wo_i),
                        (wgT, wg_i), (wuT, wu_i)]:
        cvt_src[dst.tensor.name] = _r(srcp, "(i p) e -> p i e", p=128)
    cvt_src[wd_i.tensor.name] = _r(wdT, "(i p) e -> p i e", p=128)
    slot = 0
    piece_store = {}

    free_at = {}
    state = {"slot": slot}
    SEMMAP = {"st": sem_st, "t": sem_t, "g0": sem_g[0], "g1": sem_g[1],
              "w0": sem_w[0], "w1": sem_w[1], "c0": sem_c[0], "c1": sem_c[1]}

    def _wait_free(eng, sl):
        for (s_, v_) in free_at.get(sl - 2, ()):
            E[eng].wait(SEMMAP[s_], v_)

    def ring_load(src_ap, nb, cols, cvt=None):
        sl = state["slot"]; state["slot"] += 1
        par = sl % 2
        v = ring[par][:, 0:nb * cols].rearrange("p (i e) -> p i e", i=nb)
        if cvt is None:
            _wait_free("sync", sl)
            E["sync"].dma_start(out=v, in_=src_ap, _inc=(sem_w[par], 16))
            ldt = ("w%d" % par, R.bump("w%d" % par, 16))
            free_at[sl] = []
        else:
            _wait_free("gpsimd", sl)
            E["gpsimd"].dma_start(out=v, in_=src_ap, _inc=(sem_g[par], 16))
            ldt = ("g%d" % par, R.bump("g%d" % par, 16))
            E["sync"].wait(sem_g[par], ldt[1])
            E["sync"].dma_start(out=cvt, in_=v, _inc=(sem_c[par], 16))
            free_at[sl] = [("c%d" % par, R.bump("c%d" % par, 16))]
        return sl, ldt, v

    def ring_release(sl):
        free_at[sl] = free_at.get(sl, []) + [("t", R.val("t"))]

    def emit_l1_prep():
        E["scalar"].wait(sem_a, R.val("a"))
        E["gpsimd"].wait(sem_v, v_mdone)
        E["gpsimd"].dma_start(out=stg2[:], in_=_r(guid_d, "(i p) e -> p i e", p=128),
                              _inc=(sem_gl, 16)); R.bump("gl", 16)
        gl_guid = R.val("gl")
        E["scalar"].wait(sem_gl, gl_guid)
        E["vector"].wait(sem_gl, gl_guid)
        for gi in range(GB):
            E["scalar"].activation(sq4[:, gi % NT, :], stg2[:, gi, :], AF.Square,
                                   accum_out=gn2[:, gi:gi + 1], _inc=(sem_a, 1))
            R.bump("a")
            if gi % NT == NT - 1:
                afence()
        a_gn2 = R.val("a")
        E["scalar"].wait(sem_a, a_gn2)
        E["scalar"].activation(gn2[:], gn2[:], AF.Ln, _inc=(sem_a, 1)); R.bump("a")
        afence()
        E["scalar"].activation(gn2[:], gn2[:], AF.Exp, scale=0.5, _inc=(sem_a, 1))
        a_gn = R.bump("a")
        E["vector"].wait(sem_a, a_gn)
        E["vector"].tensor_scalar_max(invg[:], gn2[:], 1e-8, _inc=(sem_v, 1)); R.bump("v")
        vfence()
        E["vector"].reciprocal(invg[:], invg[:], _inc=(sem_v, 1)); R.bump("v")
        vfence()
        for gi in range(GB):
            buf_use("vector", "gnrm")
            E["vector"].tensor_scalar(gnrm, stg2[:, gi, :], invg[:, gi:gi + 1],
                                      None, ALU.mult)
            E["vector"].tensor_copy(guidb[:, gi, :], stg2[:, gi, :], _inc=(sem_v, 1))
            v_gn = R.bump("v")
            E["tensor"].wait(sem_v, v_gn)
            psum_use("pt")
            for i in range(DB):
                kw = {} if i < DB - 1 else {"_inc": (sem_t, 1)}
                E["tensor"].transpose(pt[:, i * 128:(i + 1) * 128],
                                      gnrm[:, i * 128:(i + 1) * 128], identb[:], **kw)
            t_tp = R.bump("t")
            buf_set("gnrm", sem_t, t_tp)
            E["vector"].wait(sem_t, t_tp)
            E["vector"].tensor_copy(guid_nT[:, :, gi * 128:(gi + 1) * 128],
                                    _r(pt[:], "p (i t) -> p i t", i=DB), _inc=(sem_v, 1))
            psum_set("pt", sem_v, R.bump("v"))

        for hf in range(2):
            E["gpsimd"].wait(sem_v, R.val("v"))
            E["gpsimd"].wait(sem_a, R.val("a"))
            E["gpsimd"].dma_start(out=stg2[:],
                                in_=_r(wlT_d, "(i p) e -> p i e", p=128)[:, 4 * hf:4 * hf + 4, :],
                                _inc=(sem_gl, 16))
            mi_wl = R.bump("gl", 16)
            E["scalar"].wait(sem_gl, mi_wl)
            for q in range(4):
                i = 4 * hf + q
                E["scalar"].activation(sq4[:, q % NT, :], stg2[:, q, :], AF.Abs,
                                       accum_out=aw[:, i:i + 1], _inc=(sem_a, 1))
                R.bump("a")
                if q % NT == NT - 1:
                    afence()
        a_aw = R.val("a")
        E["vector"].wait(sem_a, a_aw)
        E["vector"].reduce_sum(gn2[:, 0:1], aw[:], axis=AX, _inc=(sem_v, 1))
        v_aw = R.bump("v")
        E["tensor"].wait(sem_v, v_aw)
        psum_use("pd")
        E["tensor"].matmul(pd[0:1, 0:1], onescb[:], gn2[:, 0:1], start=True, stop=True,
                           _inc=(sem_t, 1))
        t_tot = R.bump("t")
        E["vector"].wait(sem_t, t_tot)
        E["vector"].tensor_scalar(scl[0:1, 0:1], pd[0:1, 0:1], 1.0 / (D * D), None,
                                  ALU.mult, _inc=(sem_v, 1))
        psum_set("pd", sem_v, R.bump("v"))
        vfence()
        E["vector"].tensor_scalar_max(scl[0:1, 1:2], scl[0:1, 0:1], 1e-5, _inc=(sem_v, 1))
        R.bump("v")
        vfence()
        E["vector"].reciprocal(scl[0:1, 2:3], scl[0:1, 1:2], _inc=(sem_v, 1))
        v_scl = R.bump("v")
        E["tensor"].wait(sem_v, v_scl)
        psum_use("pd")
        E["tensor"].matmul(pd[:, 1:2], onesrb[:], scl[0:1, 1:2], start=True, stop=True)
        E["tensor"].matmul(pd[:, 2:3], onesrb[:], scl[0:1, 2:3], start=True, stop=True,
                           _inc=(sem_t, 1))
        t_bc = R.bump("t")
        E["vector"].wait(sem_t, t_bc)
        E["vector"].tensor_copy(scaleB[:], pd[:, 1:2])
        E["vector"].tensor_copy(rscB[:], pd[:, 2:3], _inc=(sem_v, 1))
        v_bc = R.bump("v")
        psum_set("pd", sem_v, v_bc)
        E["vector"].wait(sem_v, v_bc)
        for hf in range(2):
            E["gpsimd"].wait(sem_v, R.val("v"))
            E["gpsimd"].wait(sem_a, R.val("a"))
            E["gpsimd"].dma_start(out=stg2[:],
                                in_=_r(wlT_d, "(i p) e -> p i e", p=128)[:, 4 * hf:4 * hf + 4, :],
                                _inc=(sem_gl, 16))
            mi_wl = R.bump("gl", 16)
            E["vector"].wait(sem_gl, mi_wl)
            for q in range(4):
                i = 4 * hf + q
                zscr = _r(xn1T[:, 0:2, :], "p a t -> p (a t)")
                zascr = _r(xn1T[:, 2:4, :], "p a t -> p (a t)")
                E["vector"].tensor_scalar(zscr, stg2[:, q, :], rscB[:, 0:1], None,
                                          ALU.mult, _inc=(sem_v, 1))
                v_z = R.bump("v")
                E["scalar"].wait(sem_v, v_z)
                E["scalar"].activation(difs[0], zscr, AF.Sign)
                E["scalar"].activation(zascr, zscr, AF.Abs, _inc=(sem_a, 1))
                a_as = R.bump("a")
                E["vector"].wait(sem_a, a_as)
                E["vector"].tensor_single_scalar(g1t, zascr, 0.5, ALU.is_gt,
                                                 _inc=(sem_v, 1))
                R.bump("v")
                vfence()
                E["vector"].tensor_mul(wl_tern[:, i, :], g1t, difs[0], _inc=(sem_v, 1))
                R.bump("v")
                vfence()


    def ln_sq_tile(xcur, j, off, sqslot):
        cs = off + 8 * j
        sqs_ = sq4[:, sqslot, :] if sqslot < NT else sq4[:, 2 + (sqslot - NT), :]
        E["scalar"].activation(sqs_, xcur[:, j, :], AF.Square,
                               accum_out=stats[:, cs + 1:cs + 2], _inc=(sem_a, 1))
        E["vector"].reduce_sum(stats[:, cs + 0:cs + 1], xcur[:, j, :], axis=AX,
                               _inc=(sem_v, 1))
        return R.bump("a"), R.bump("v")

    def ln_finish(off):
        E["vector"].wait(sem_v, R.val("v"))
        E["vector"].wait(sem_a, R.val("a"))
        for j in range(NT):
            cs = off + 8 * j
            kw = {} if j < NT - 1 else {"_inc": (sem_v, 1)}
            E["vector"].tensor_scalar(stats[:, cs + 2:cs + 3], stats[:, cs + 0:cs + 1],
                                      -1.0 / D, None, ALU.mult, **kw)
        v_mneg = R.bump("v")
        E["vector"].wait(sem_v, v_mneg)
        for j in range(NT):
            cs = off + 8 * j
            kw = {} if j < NT - 1 else {"_inc": (sem_v, 1)}
            E["vector"].scalar_tensor_tensor(stats[:, cs + 3:cs + 4],
                                             stats[:, cs + 0:cs + 1],
                                             stats[:, cs + 2:cs + 3],
                                             stats[:, cs + 1:cs + 2],
                                             ALU.mult, ALU.add, **kw)
        v_var = R.bump("v")
        E["scalar"].wait(sem_v, v_var)
        var_cols = _r(stats[:, off:off + 8 * NT], "p (j k) -> p j k", j=NT)[:, :, 3:4]
        E["scalar"].activation(var_cols, var_cols, AF.Ln, scale=1.0 / D,
                               bias=cvec[:, 0:1], _inc=(sem_a, 1))
        a_ln = R.bump("a")
        E["scalar"].wait(sem_a, a_ln)
        E["scalar"].activation(var_cols, var_cols, AF.Exp, scale=-0.5, _inc=(sem_a, 1))
        return R.bump("a")

    def ln_apply(xcur, off, xnT, a_rstd):
        E["vector"].wait(sem_a, a_rstd)
        xnT_done = None
        for j in range(NT):
            cs = off + 8 * j
            E["vector"].tensor_scalar(xnt[:], xcur[:, j, :], stats[:, cs + 2:cs + 3],
                                      stats[:, cs + 3:cs + 4], ALU.add, ALU.mult,
                                      _inc=(sem_v, 1))
            v_apply = R.bump("v")
            E["tensor"].wait(sem_v, v_apply)
            psum_use("pt")
            for i in range(DB):
                kw = {} if i < DB - 1 else {"_inc": (sem_t, 1)}
                E["tensor"].transpose(pt[:, i * 128:(i + 1) * 128],
                                      xnt[:, i * 128:(i + 1) * 128], identb[:], **kw)
            t_tp = R.bump("t")
            E["vector"].wait(sem_t, t_tp)
            E["vector"].tensor_copy(xnT[:, :, j * 128:(j + 1) * 128],
                                    _r(pt[:], "p (i t) -> p i t", i=DB),
                                    _inc=(sem_v, 1))
            xnT_done = R.bump("v")
            psum_set("pt", sem_v, xnT_done)
        return xnT_done

    pt_seq = [0]
    carry_v = {}

    def tile_pre(dcm, idx):
        E["tensor"].wait(sem_v, R.val("v"))
        psum_use("pt")
        buf_use("tensor", "dif%d" % idx)
        for i in range(DB):
            kw = {} if i < DB - 1 else {"_inc": (sem_t, 1)}
            E["tensor"].transpose(pt[:, i * 128:(i + 1) * 128], dcm[:, i, :],
                                  identb[:], **kw)
        t_tp = R.bump("t")
        buf_use("vector", "difv%d" % idx)
        E["vector"].wait(sem_t, t_tp)
        E["vector"].tensor_copy(difs[idx], pt[:], _inc=(sem_v, 1))
        v_dif = R.bump("v")
        psum_set("pt", sem_v, v_dif)
        E["scalar"].wait(sem_v, v_dif)
        buf_use("scalar", "xnt_sq")
        E["scalar"].activation(xnt[:], difs[idx], AF.Square,
                               accum_out=statL[:, 0 + idx:1 + idx], _inc=(sem_a, 1))
        a_m2 = R.bump("a")
        buf_set("xnt_sq", sem_a, a_m2)
        return a_m2

    def stats_batch(nw):
        c0, c1_, c2_, c3, c4, c5, c6, c7 = (statL[:, 0:nw], statL[:, 4:4 + nw],
                                            statL[:, 8:8 + nw], statL[:, 12:12 + nw],
                                            statL[:, 16:16 + nw], statL[:, 20:20 + nw],
                                            statL[:, 24:24 + nw], statL[:, 28:28 + nw])
        E["vector"].wait(sem_a, R.val("a"))
        E["vector"].tensor_scalar_max(c1_, c0, 1e-24, _inc=(sem_v, 1))
        v1 = R.bump("v")
        E["scalar"].wait(sem_v, v1)
        E["scalar"].activation(c2_, c1_, AF.Ln, _inc=(sem_a, 1))
        aln = R.bump("a")
        E["scalar"].wait(sem_a, aln)
        E["scalar"].activation(c3, c2_, AF.Exp, scale=0.5)
        E["scalar"].activation(c4, c2_, AF.Exp, scale=-0.5, _inc=(sem_a, 1))
        amag = R.bump("a")
        E["scalar"].wait(sem_a, amag)
        E["scalar"].activation(c5, c3, AF.Exp, scale=-4.0, _inc=(sem_a, 1))
        au = R.bump("a")
        E["vector"].wait(sem_a, au)
        E["vector"].tensor_single_scalar(c6, c0, 1e-12, ALU.is_gt)
        E["vector"].tensor_scalar(c7, c5, 1.0, None, ALU.add, _inc=(sem_v, 1))
        v2 = R.bump("v")
        E["vector"].wait(sem_v, v2)
        E["vector"].tensor_mul(c4, c4, c6)
        E["vector"].reciprocal(c7, c7, _inc=(sem_v, 1))
        v3 = R.bump("v")
        E["vector"].wait(sem_v, v3)
        E["vector"].tensor_scalar(c5, c5, -1.0, 1.0, ALU.mult, ALU.add)
        E["vector"].tensor_scalar(c6, c4, 2.0, None, ALU.mult, _inc=(sem_v, 1))
        v4 = R.bump("v")
        E["vector"].wait(sem_v, v4)
        E["vector"].tensor_mul(c5, c5, c7, _inc=(sem_v, 1))
        v5 = R.bump("v")
        E["vector"].wait(sem_v, v5)
        E["vector"].scalar_tensor_tensor(c2_, c5, 0.6, c4, ALU.mult, ALU.mult)
        E["vector"].tensor_scalar(c3, c5, 0.4, None, ALU.mult)
        buf_use("vector", "mwb4")
        E["vector"].tensor_copy(mwb4[:, 1:1 + nw], c5, _inc=(sem_v, 1))
        return R.bump("v")

    def dens_batch(nw, v_stats, first_tile_mfirst):
        E["tensor"].wait(sem_v, v_stats)
        psum_use("pd")
        if first_tile_mfirst:
            E["tensor"].matmul(pd[:, 1:2], mfirstb[:], mwb4[:, 1:2],
                               start=True, stop=True)
            E["tensor"].matmul(pd[:, 2:1 + nw], mbandb[:], mwb4[:, 2:1 + nw],
                               start=True, stop=False)
            E["tensor"].matmul(pd[:, 2:1 + nw], mprevb[:], mwb4[:, 1:nw],
                               start=False, stop=True, _inc=(sem_t, 1))
        else:
            E["tensor"].matmul(pd[:, 1:1 + nw], mbandb[:], mwb4[:, 1:1 + nw],
                               start=True, stop=False)
            E["tensor"].matmul(pd[:, 1:1 + nw], mprevb[:], mwb4[:, 0:nw],
                               start=False, stop=True, _inc=(sem_t, 1))
        t_den = R.bump("t")
        E["vector"].wait(sem_t, t_den)
        E["vector"].tensor_scalar_max(statL[:, 36:36 + nw], pd[:, 1:1 + nw], 1e-8,
                                      _inc=(sem_v, 1))
        v_dmax = R.bump("v")
        psum_set("pd", sem_v, v_dmax)
        E["vector"].wait(sem_v, v_dmax)
        E["vector"].reciprocal(statL[:, 36:36 + nw], statL[:, 36:36 + nw],
                               _inc=(sem_v, 1))
        return R.bump("v")

    def tile_dots(idx, dcm, v_stats):
        psum_use("pd")
        for i in range(DB):
            kw = {} if i < DB - 1 else {"_inc": (sem_t, 1)}
            E["tensor"].matmul(pd[:], dcm[:, i, :], guid_nT[:, i, :],
                               start=(i == 0), stop=(i == DB - 1), **kw)
        t_dots = R.bump("t")
        E["scalar"].wait(sem_t, t_dots)
        E["scalar"].wait(sem_v, v_stats)
        buf_use("scalar", "exp_sb")
        E["scalar"].activation(exp_sb, pd[:], AF.Exp,
                               scale=statL[:, 24 + idx:25 + idx],
                               accum_out=statL[:, 28 + idx:29 + idx], _inc=(sem_a, 1))
        a_exp = R.bump("a")
        psum_set("pd", sem_a, a_exp)
        E["vector"].wait(sem_a, a_exp)
        E["vector"].reciprocal(statL[:, 32 + idx:33 + idx],
                               statL[:, 28 + idx:29 + idx], _inc=(sem_v, 1))
        v_rs = R.bump("v")
        E["vector"].wait(sem_v, v_rs)
        E["vector"].tensor_mul(statL[:, 32 + idx:33 + idx],
                               statL[:, 32 + idx:33 + idx],
                               statL[:, 12 + idx:13 + idx], _inc=(sem_v, 1))
        v_c2r = R.bump("v")
        return a_exp, v_c2r

    def tile_gw(idx, a_exp):
        E["tensor"].wait(sem_a, a_exp)
        psum_use("pt")
        for gi in range(GB):
            kw = {} if gi < GB - 1 else {"_inc": (sem_t, 1)}
            E["tensor"].transpose(pt[:, gi * 128:(gi + 1) * 128],
                                  exp_sb[:, gi * 128:(gi + 1) * 128], identb[:], **kw)
        t_gwt = R.bump("t")
        buf_set("exp_sb", sem_t, t_gwt)
        buf_use("vector", "gwT")
        E["vector"].wait(sem_t, t_gwt)
        E["vector"].tensor_copy(gwT[:], _r(pt[:, 0:512], "p (i t) -> p i t", i=GB),
                                _inc=(sem_v, 1))
        v_gwT = R.bump("v")
        psum_set("pt", sem_v, v_gwT)
        E["tensor"].wait(sem_v, v_gwT)
        psum_use("pa")
        for gi in range(GB):
            for h2 in range(2):
                kw = {} if not (gi == GB - 1 and h2 == 1) else {"_inc": (sem_t, 1)}
                E["tensor"].matmul(pa[:, h2 * 512:(h2 + 1) * 512], gwT[:, gi, :],
                                   guidb[:, gi, h2 * 512:(h2 + 1) * 512],
                                   start=(gi == 0), stop=(gi == GB - 1), **kw)
        t_inf = R.bump("t")
        buf_set("gwT", sem_t, t_inf)
        return t_inf

    def tile_t2blend(idx, sl, t_inf, v_c2r):
        bwc = bw[sl]
        E["vector"].wait(sem_t, t_inf)
        E["vector"].wait(sem_v, v_c2r)
        buf_use("vector", "t2")
        E["vector"].tensor_scalar(t2, pa[:], statL[:, 32 + idx:33 + idx], None,
                                  ALU.mult, _inc=(sem_v, 1))
        v_t2 = R.bump("v")
        psum_set("pa", sem_v, v_t2)
        E["vector"].wait(sem_v, v_t2)
        buf_use("vector", "bw%d" % sl)
        E["vector"].scalar_tensor_tensor(bwc[:], difs[idx], statL[:, 8 + idx:9 + idx],
                                         t2, ALU.mult, ALU.add, _inc=(sem_v, 1))
        v_bw = R.bump("v")
        buf_set("difv%d" % idx, sem_v, v_bw)
        buf_set("t2", sem_v, v_bw)
        return v_bw

    def tile_numf1(idx, sl, dcm, x1src, xnTsrc, mmat, use_prev, out_rows,
                   v_rden, v_bw):
        bwc, bwh = bw[sl], bw[1 - sl]
        E["tensor"].wait(sem_v, v_bw)
        psum_use("pb")
        for h2 in range(2):
            kw = {} if (use_prev or h2 == 0) else {"_inc": (sem_t, 1)}
            E["tensor"].matmul(pb[:, h2 * 512:(h2 + 1) * 512], mmat[:],
                               bwc[:, h2 * 512:(h2 + 1) * 512],
                               start=True, stop=(not use_prev), **kw)
        if use_prev:
            for h2 in range(2):
                kw = {} if h2 == 0 else {"_inc": (sem_t, 1)}
                E["tensor"].matmul(pb[:, h2 * 512:(h2 + 1) * 512], mprevb[:],
                                   bwh[:, h2 * 512:(h2 + 1) * 512],
                                   start=False, stop=True, **kw)
        t_num = R.bump("t")
        buf_set("bw%d" % (1 - sl), sem_t, t_num)
        psum_use("pc")
        for i in range(DB):
            for h2 in range(2):
                kw = {} if not (i == DB - 1 and h2 == 1) else {"_inc": (sem_t, 1)}
                E["tensor"].matmul(pc[:, h2 * 512:(h2 + 1) * 512], xnTsrc[:, i, :],
                                   wl_tern[:, i, h2 * 512:(h2 + 1) * 512],
                                   start=(i == 0), stop=(i == DB - 1), **kw)
        t_f1 = R.bump("t")
        E["scalar"].wait(sem_t, t_f1)
        buf_use("scalar", "g1t")
        E["scalar"].activation(g1t, pc[:], AF.Silu, scale=scaleB[:, 0:1],
                               _inc=(sem_a, 1))
        a_g1 = R.bump("a")
        psum_set("pc", sem_a, a_g1)
        E["vector"].wait(sem_v, v_rden)
        E["vector"].wait(sem_t, t_num)
        buf_use("vector", "tr1")
        E["vector"].tensor_scalar(tr1, pb[:], statL[:, 36 + idx:37 + idx], None,
                                  ALU.mult, _inc=(sem_v, 1))
        v_tr1 = R.bump("v")
        psum_set("pb", sem_v, v_tr1)
        E["gpsimd"].wait(sem_v, v_tr1)
        E["gpsimd"].wait(sem_a, a_g1)
        buf_use("gpsimd", "tg")
        E["gpsimd"].tensor_mul(tg, tr1, g1t, _inc=(sem_gc, 1))
        g_tg = R.bump("gc")
        buf_set("tr1", sem_gc, g_tg)
        buf_set("g1t", sem_gc, g_tg)
        E["vector"].wait(sem_gc, g_tg)
        E["vector"].tensor_add(tg, tg, x1src, _inc=(sem_v, 1))
        v_x2 = R.bump("v")
        E["sync"].wait(sem_v, v_x2)
        E["sync"].dma_start(out=_r(out_d[out_rows:out_rows + 128, :], "(o p) e -> p (o e)", p=128),
                            in_=tg, _inc=(sem_st, 16))
        st_c = R.bump("st", 16)
        buf_set("tg", sem_st, st_c)

    def tile_pipeline(tiles, v_stats, v_rden, post0=None):
        n = len(tiles)
        st = {0: tile_dots(0, tiles[0][0], v_stats)}
        ti = {0: tile_gw(0, st[0][0])}
        for idx in range(n):
            sl = pt_seq[0] % 2
            v_bw = tile_t2blend(idx, sl, ti[idx], st[idx][1])
            if idx + 1 < n:
                st[idx + 1] = tile_dots(idx + 1, tiles[idx + 1][0], v_stats)
            dcm, x1src, xnTsrc, mmat, use_prev, out_rows = tiles[idx]
            tile_numf1(idx, sl, dcm, x1src, xnTsrc, mmat, use_prev, out_rows,
                       v_rden, v_bw)
            if idx == 0 and post0 is not None:
                post0()
            if idx + 1 < n:
                ti[idx + 1] = tile_gw(idx + 1, st[idx + 1][0])
            pt_seq[0] += 1

    sq_fc = {}
    rstd0 = None
    for c in range(NCH):
        xcur = xa if c % 2 == 0 else xb
        xnext = xb if c % 2 == 0 else xa
        if c + 1 < NCH:
            E["sync"].wait(sem_st, R.val("st"))
            if c - 1 in carry_v:
                E["sync"].wait(sem_v, carry_v[c - 1])
            par = (c + 1) % 2
            E["sync"].dma_start(
                out=xnext[:],
                in_=_r(x_in[(c + 1) * T:(c + 2) * T, :], "(j p) e -> p j e", p=128),
                _inc=(sem_x[par], 16))
            x_ld[c + 1] = (par, R.bump("x%d" % par if par == 0 else "x1s", 16))

        if c == 0:
            E["scalar"].wait(sem_x[x_ld[c][0]], x_ld[c][1])
            E["scalar"].wait(sem_v, R.val("v"))
            E["vector"].wait(sem_x[x_ld[c][0]], x_ld[c][1])
            for j in range(NT):
                ln_sq_tile(xcur, j, 0, j)
            rstd0 = ln_finish(0)
        xn0T_done = ln_apply(xcur, 0, xn0T, rstd0)
        E["tensor"].wait(sem_v, xn0T_done)

        def kvr(w_int, mode, dst):
            cv = None if c > 0 else w_int
            srcap = w_int if c > 0 else cvt_src[w_int.tensor.name]
            sl_, ldt, wv = ring_load(srcap, DB, D, cvt=cv)
            E["tensor"].wait(SEMMAP[ldt[0]], ldt[1])
            evac = {}
            for eb in range(DB):
                key = "pa" if eb % 2 == 0 else "pb"
                reg = (pa if eb % 2 == 0 else pb)[:, 0:T]
                psum_use(key)
                for i in range(DB):
                    kw = {} if i < DB - 1 else {"_inc": (sem_t, 1)}
                    E["tensor"].matmul(reg, wv[:, i, eb * 128:(eb + 1) * 128],
                                       xn0T[:, i, :], start=(i == 0),
                                       stop=(i == DB - 1), **kw)
                tmm = R.bump("t")
                if mode == "s_copy":
                    E["scalar"].wait(sem_t, tmm)
                    E["scalar"].copy(dst[:, eb, :], reg, _inc=(sem_a, 1))
                    evac[eb] = R.bump("a")
                    psum_set(key, sem_a, evac[eb])
                elif mode == "s_tanh":
                    E["scalar"].wait(sem_t, tmm)
                    E["scalar"].activation(dst[:, eb, :], reg, AF.Tanh, scale=0.5,
                                           _inc=(sem_a, 1))
                    evac[eb] = R.bump("a")
                    psum_set(key, sem_a, evac[eb])
                else:
                    E["vector"].wait(sem_t, tmm)
                    E["vector"].tensor_copy(dst[:, eb, :], reg, _inc=(sem_v, 1))
                    evac[eb] = R.bump("v")
                    psum_set(key, sem_v, evac[eb])
            ring_release(sl_)
            return evac[DB - 1]

        k_done = kvr(wk_i, "s_copy", kbf)
        v_done = kvr(wv_i, "v_copy", vbf)
        r_done = kvr(wr_i, "s_tanh", trs)
        E["vector"].wait(sem_v, v_done)

        E["vector"].wait(sem_st, R.val("st"))
        E["vector"].wait(sem_a, k_done)
        E["vector"].tensor_mul(kbf[:], kbf[:], vbf[:], _inc=(sem_v, 1))
        v_kv = R.bump("v")
        E["vector"].wait(sem_v, v_kv)
        for i in range(DB):
            kw = {} if i < DB - 1 else {"_inc": (sem_v, 1)}
            E["vector"].tensor_tensor_scan(
                zs[:, i, 0:T], dec[:, i:i + 1].broadcast_to([128, T]),
                kbf[:, i, :], zcarry[:, i, 0:1], ALU.mult, ALU.add, **kw)
        v_scan = R.bump("v")
        E["vector"].wait(sem_v, v_scan)
        for i in range(DB):
            E["vector"].scalar_tensor_tensor(
                vbf[:, i, 0:1], kbf[:, i, 0:1], eu[:, i:i + 1], zcarry[:, i, 0:1],
                ALU.mult, ALU.add)
            kw = {} if i < DB - 1 else {"_inc": (sem_v, 1)}
            E["vector"].scalar_tensor_tensor(
                vbf[:, i, 1:T], kbf[:, i, 1:T], eu[:, i:i + 1], zs[:, i, 0:T - 1],
                ALU.mult, ALU.add, **kw)
        v_wkv = R.bump("v")
        E["vector"].wait(sem_v, v_wkv)
        E["vector"].tensor_copy(zcarry[:], zs[:, :, T - 1:T])
        E["vector"].wait(sem_a, r_done)
        E["vector"].scalar_tensor_tensor(trs[:], trs[:], 1.0, vbf[:], ALU.add,
                                         ALU.mult, _inc=(sem_v, 1))
        s0_done = R.bump("v")

        E["vector"].wait(sem_st, R.val("st"))
        for q in range(2):
            colr = slice(q * HID // 2, (q + 1) * HID // 2)
            sl_u, ldt, wuv = ring_load(
                wu_i[:, :, colr] if c > 0 else cvt_src[wu_i.tensor.name][:, :, colr],
                DB, HID // 2, cvt=None if c > 0 else wu_i[:, :, colr])
            E["tensor"].wait(SEMMAP[ldt[0]], ldt[1])
            upev = {}
            for hb in range(HQ):
                key = "pa" if hb % 2 == 0 else "pb"
                reg = (pa if hb % 2 == 0 else pb)[:, 0:T]
                psum_use(key)
                for i in range(DB):
                    kw = {} if i < DB - 1 else {"_inc": (sem_t, 1)}
                    E["tensor"].matmul(reg, wuv[:, i, hb * 128:(hb + 1) * 128],
                                       xn0T[:, i, :], start=(i == 0),
                                       stop=(i == DB - 1), **kw)
                tmm = R.bump("t")
                if hb == 0 and q >= 1:
                    E["vector"].wait(sem_v, buf_guard.get("sgt_q", (sem_v, 0))[1])
                E["vector"].wait(sem_t, tmm)
                E["vector"].tensor_copy(up_sb[:, hb, :], reg, _inc=(sem_v, 1))
                upev[hb] = R.bump("v")
                psum_set(key, sem_v, upev[hb])
            ring_release(sl_u)
            sl_g, ldt, wgv = ring_load(
                wg_i[:, :, colr] if c > 0 else cvt_src[wg_i.tensor.name][:, :, colr],
                DB, HID // 2, cvt=None if c > 0 else wg_i[:, :, colr])
            E["tensor"].wait(SEMMAP[ldt[0]], ldt[1])
            sgc = {}
            hdone = {}
            for hb in range(HQ):
                key = "pa" if hb % 2 == 0 else "pb"
                reg = (pa if hb % 2 == 0 else pb)[:, 0:T]
                psum_use(key)
                for i in range(DB):
                    kw = {} if i < DB - 1 else {"_inc": (sem_t, 1)}
                    E["tensor"].matmul(reg, wgv[:, i, hb * 128:(hb + 1) * 128],
                                       xn0T[:, i, :], start=(i == 0),
                                       stop=(i == DB - 1), **kw)
                tmm = R.bump("t")
                if hb >= 1:
                    E["scalar"].wait(sem_v, hdone[hb - 1])
                elif q >= 1:
                    E["scalar"].wait(sem_v, buf_guard.get("sgt_q", (sem_v, 0))[1])
                E["scalar"].wait(sem_t, tmm)
                E["scalar"].activation(sgt, reg, AF.Silu, scale=1.0 / 127.0,
                                       _inc=(sem_a, 1))
                sgc[hb] = R.bump("a")
                psum_set(key, sem_a, sgc[hb])
                E["vector"].wait(sem_a, sgc[hb])
                E["vector"].tensor_mul(h_sb[:, q * HQ + hb, :], sgt, up_sb[:, hb, :],
                                       _inc=(sem_v, 1))
                hdone[hb] = R.bump("v")
            buf_set("sgt_q", sem_v, hdone[HQ - 1])
            ring_release(sl_g)
        h_all = R.val("v")

        sl_o, ldt, wov = ring_load(wo_i if c > 0 else cvt_src[wo_i.tensor.name],
                                   DB, D, cvt=None if c > 0 else wo_i)
        E["tensor"].wait(SEMMAP[ldt[0]], ldt[1])
        E["tensor"].wait(sem_v, s0_done)
        E["tensor"].wait(sem_v, h_all)
        s_evac = {}
        for j in range(NT):
            key = "pa" if j % 2 == 0 else "pb"
            reg = (pa if j % 2 == 0 else pb)[:, 0:1024]
            psum_use(key)
            for i in range(DB):
                for h2 in range(2):
                    kw = {} if not (i == DB - 1 and h2 == 1) else {"_inc": (sem_t, 1)}
                    E["tensor"].matmul(reg[:, h2 * 512:(h2 + 1) * 512],
                                       trs[:, i, j * 128:(j + 1) * 128],
                                       wov[:, i, h2 * 512:(h2 + 1) * 512],
                                       start=(i == 0), stop=(i == DB - 1), **kw)
            tmm = R.bump("t")
            E["vector"].wait(sem_t, tmm)
            E["vector"].tensor_copy(s_bf[:, j, :], reg, _inc=(sem_v, 1))
            s_evac[j] = R.bump("v")
            psum_set(key, sem_v, s_evac[j])
        ring_release(sl_o)

        sl_d0, ldt0, wdv0 = ring_load(
            wd_i[:, 0:HQ, :] if c > 0 else cvt_src[wd_i.tensor.name][:, 0:HQ, :],
            HQ, D, cvt=None if c > 0 else wd_i[:, 0:HQ, :])
        sl_d1, ldt1, wdv1 = ring_load(
            wd_i[:, HQ:HB, :] if c > 0 else cvt_src[wd_i.tensor.name][:, HQ:HB, :],
            HQ, D, cvt=None if c > 0 else wd_i[:, HQ:HB, :])
        E["tensor"].wait(SEMMAP[ldt0[0]], ldt0[1])
        E["tensor"].wait(SEMMAP[ldt1[0]], ldt1[1])
        E["tensor"].wait(sem_v, h_all)
        pregs = [("pa", pa), ("pb", pb), ("pc", pc)]
        for j in range(NT):
            key, reg = pregs[j % 3]
            psum_use(key)
            for hb in range(HB):
                wdv = wdv0 if hb < HQ else wdv1
                hbl = hb % HQ
                for h2 in range(2):
                    last = (hb == HB - 1 and h2 == 1)
                    kw = {} if not last else {"_inc": (sem_t, 1)}
                    E["tensor"].matmul(reg[:, h2 * 512:(h2 + 1) * 512],
                                       h_sb[:, hb, j * 128:(j + 1) * 128],
                                       wdv[:, hbl, h2 * 512:(h2 + 1) * 512],
                                       start=(hb == 0), stop=(hb == HB - 1), **kw)
            t_down = R.bump("t")
            g0t_j, t1t_j = g0ts[j % 2], t1ts[j % 2]
            E["scalar"].wait(sem_t, t_down)
            buf_use("scalar", "g0t%d" % (j % 2))
            E["scalar"].activation(g0t_j, reg[:], AF.Silu, scale=1.0 / 16129.0,
                                   _inc=(sem_a, 1))
            ag0 = R.bump("a")
            psum_set(key, sem_a, ag0)
            E["gpsimd"].wait(sem_a, ag0)
            E["gpsimd"].wait(sem_v, s_evac[j])
            buf_use("gpsimd", "t1t%d" % (j % 2))
            E["gpsimd"].tensor_mul(t1t_j, s_bf[:, j, :], g0t_j, _inc=(sem_gc, 1))
            gmul = R.bump("gc")
            buf_set("g0t%d" % (j % 2), sem_gc, gmul)
            E["vector"].wait(sem_gc, gmul)
            E["vector"].scalar_tensor_tensor(xcur[:, j, :], t1t_j, 0.5,
                                             xcur[:, j, :], ALU.mult, ALU.add,
                                             _inc=(sem_v, 1))
            vx1 = R.bump("v")
            buf_set("t1t%d" % (j % 2), sem_v, vx1)
            E["scalar"].wait(sem_v, vx1)
            E["vector"].wait(sem_v, vx1)
            ln_sq_tile(xcur, j, 32, j)
        ring_release(sl_d0)
        ring_release(sl_d1)

        if c == 0:
            emit_l1_prep()

        rstd1 = ln_finish(32)
        xn1T_done = ln_apply(xcur, 32, xn1T, rstd1)
        E["vector"].wait(sem_v, xn1T_done)
        if c > 0:
            E["vector"].tensor_sub(deprev[:, :, 127:128], xn1T[:, :, 0:1],
                                   xn1last[:, :, 0:1])
        buf_use("vector", "de")
        E["vector"].tensor_sub(de[:, :, 0:T - 1], xn1T[:, :, 1:T],
                               xn1T[:, :, 0:T - 1])
        E["vector"].memset(de[:, :, T - 1:T], 0.0, _inc=(sem_v, 1))
        R.bump("v")

        tiles = []
        if c > 0:
            tiles.append((deprev[:], x1prev[:], xn1Tprev[:], mbandb, True,
                          (c - 1) * T + (NT - 1) * 128))
        for jt in range(NT - 1):
            mm = mfirstb if (c == 0 and jt == 0) else mbandb
            useprev = not (c == 0 and jt == 0)
            tiles.append((de[:, :, jt * 128:(jt + 1) * 128], xcur[:, jt, :],
                          xn1T[:, :, jt * 128:(jt + 1) * 128], mm, useprev,
                          c * T + jt * 128))
        for idx, tl in enumerate(tiles):
            tile_pre(tl[0], idx)
        v_stats = stats_batch(len(tiles))
        v_rden = dens_batch(len(tiles), v_stats, c == 0)
        E["vector"].wait(sem_t, R.val("t"))
        E["vector"].tensor_copy(mwb4[:, 0:1], mwb4[:, len(tiles):len(tiles) + 1],
                                _inc=(sem_v, 1))
        R.bump("v")

        def post0():
            E["vector"].wait(sem_t, R.val("t"))
            E["vector"].tensor_copy(deprev[:], de[:, :, (NT - 1) * 128:T])
            E["vector"].tensor_copy(xn1last[:], xn1T[:, :, T - 1:T])
            E["vector"].tensor_copy(xn1Tprev[:], xn1T[:, :, (NT - 1) * 128:T])
            E["vector"].tensor_copy(x1prev[:], xcur[:, NT - 1, :],
                                    _inc=(sem_v, 1))
            carry_v[c] = R.bump("v")
            buf_set("de", sem_v, carry_v[c])
        tile_pipeline(tiles, v_stats, v_rden, post0=post0)

        if c + 1 < NCH:
            xnx = xb if c % 2 == 0 else xa
            E["scalar"].wait(sem_x[x_ld[c + 1][0]], x_ld[c + 1][1])
            E["vector"].wait(sem_x[x_ld[c + 1][0]], x_ld[c + 1][1])
            for j in range(NT):
                if j >= 2:
                    E["scalar"].wait(sem_a, sq_fc[j - 2])
                sq_fc[j] = ln_sq_tile(xnx, j, 0, NT + (j % 2))[0]
            rstd0 = ln_finish(0)

    tile_pre(deprev[:], 0)
    v_stats = stats_batch(1)
    v_rden = dens_batch(1, v_stats, False)
    tile_pipeline([(deprev[:], x1prev[:], xn1Tprev[:], mbandb, True,
                    (NCH - 1) * T + (NT - 1) * 128)], v_stats, v_rden)
    E["sync"].wait(sem_st, R.val("st"))

    with es:
        block = es.enter_context(nc.Block())
        block.sync(lambda e: [fn(e) for fn in R.prog["sync"]])
        block.gpsimd(lambda e: [fn(e) for fn in R.prog["gpsimd"]])
        block.tensor(lambda e: [fn(e) for fn in R.prog["tensor"]])
        block.vector(lambda e: [fn(e) for fn in R.prog["vector"]])
        block.scalar(lambda e: [fn(e) for fn in R.prog["scalar"]])
    return nc


def host_prep(inputs):
    f32 = np.float32
    Mband, Mprev, Mfirst = make_M_matrices()
    M64b, M64p, M64f = make_M_matrices_f64()
    Mlo = np.concatenate([(M64b - Mband.astype(np.float64)).astype(f32),
                          (M64p - Mprev.astype(np.float64)).astype(f32),
                          (M64f - Mfirst.astype(np.float64)).astype(f32)], axis=1)
    return {
        "wkT": np.ascontiguousarray(np.asarray(inputs["L0_attn_Wk"], f32).T),
        "wvT": np.ascontiguousarray(np.asarray(inputs["L0_attn_Wv"], f32).T),
        "wrT": np.ascontiguousarray(np.asarray(inputs["L0_attn_Wr"], f32).T),
        "woT": np.ascontiguousarray(np.asarray(inputs["L0_attn_Wo"], f32).T),
        "wgT": np.ascontiguousarray(np.asarray(inputs["L0_ffn_w_gate"], np.int32).T),
        "wuT": np.ascontiguousarray(np.asarray(inputs["L0_ffn_w_up"], np.int32).T),
        "wdT": np.ascontiguousarray(np.asarray(inputs["L0_ffn_w_down"], np.int32).T),
        "td_c": np.ascontiguousarray(np.asarray(inputs["L0_attn_time_decay"], f32).reshape(DB, 128).T),
        "tf_c": np.ascontiguousarray(np.asarray(inputs["L0_attn_time_first"], f32).reshape(DB, 128).T),
        "guid": np.ascontiguousarray(np.asarray(inputs["L1_attn_guidance_trajectories"], f32)),
        "wlT": np.ascontiguousarray(np.asarray(inputs["L1_ffn_weight"], f32).T),
        "ident": np.eye(128, dtype=ml_dtypes.bfloat16),
        "cvec_in": np.broadcast_to(np.array([1e-5, np.log(2.0), 0.0, 0.0], f32), (128, 4)).copy(),
        "mband": Mband, "mprev": Mprev, "mfirst": Mfirst,
        "mlo": Mlo,
        "onesc": np.ones((128, 1), f32), "onesr": np.ones((1, 128), f32),
    }


_NC_CACHE = {}


def kernel(**inputs):
    x = np.asarray(inputs["L0_attn_x"], np.float32)
    B = x.shape[0]
    if "nc" not in _NC_CACHE:
        _NC_CACHE["nc"] = build_fused()
    nc = _NC_CACHE["nc"]
    shared = host_prep(inputs)
    in_maps = []
    for b in range(B):
        m = dict(shared)
        m["x"] = np.ascontiguousarray(x[b])
        in_maps.append(m)
    res = run_bass_kernel_spmd(nc, in_maps, core_ids=list(range(B)))
    return np.stack([r["out"] for r in res.results], axis=0)








